# version 07
# speedup vs baseline: 1.3765x; 1.0054x over previous
import numpy as np
import ml_dtypes
from contextlib import ExitStack

import concourse.bass as bass
import concourse.mybir as mybir
from concourse.bass_utils import run_bass_kernel_spmd

F32 = mybir.dt.float32
BF16 = mybir.dt.bfloat16
ALU = mybir.AluOpType
AF = mybir.ActivationFunctionType
AX = mybir.AxisListType

T = 1024
TH = T + 3
C = 2048
KC = 16
NH = 8192
EPS = 1e-5
BW = 16 * 1028
RSQ = float(1.0 / np.sqrt(128.0))
SCL = float(1.0 / np.sqrt(NH))

PHASE_LIMIT = 4
FINE = False
DBG = []
FENCES = []


class Ctx:
    def __init__(self, nc, stack):
        self.nc = nc
        self.stack = stack
        self.sems = {}
        self.cnt = {}

    def s(self, name):
        if name not in self.sems:
            self.sems[name] = self.stack.enter_context(self.nc.semaphore(name))
            self.cnt[name] = 0
        return self.sems[name]

    def inc(self, inst, name, n=1):
        inst.then_inc(self.s(name), n)
        self.cnt[name] += n
        return self.cnt[name]

    def wait(self, eng, name, val=None):
        self.s(name)
        v = self.cnt[name] if val is None else val
        if v > 0:
            eng.wait_ge(self.sems[name], v)

    def val(self, name):
        self.s(name)
        return self.cnt[name]

    def base(self, *names):
        return {n: self.val(n) for n in names}

    def fence(self, engines, names):
        for name in names:
            self.s(name)
            if self.cnt[name] > 0:
                for e in engines:
                    e.wait_ge(self.sems[name], self.cnt[name])


def _had(n):
    h = np.array([[1.0]], dtype=np.float32)
    while h.shape[0] < n:
        h = np.block([[h, h], [h, -h]])
    return h.astype(np.float32)


def _perm_jnew():
    p = np.zeros(NH, dtype=np.int64)
    for c in range(64):
        for d in range(128):
            p[d * 64 + c] = c * 128 + d
    return p


def host_inputs(inp):
    f32 = np.float32
    xg = np.asarray(inp["L0_attn_a_x"], dtype=f32)

    def tr(a):
        return np.ascontiguousarray(np.asarray(a, f32).T)

    def slabbed(WT_ko, ocn):
        K, O = WT_ko.shape
        kcn = K // 128
        a = WT_ko.reshape(kcn, 128, ocn, O // ocn)
        return np.ascontiguousarray(a.transpose(2, 1, 0, 3).reshape(ocn, 128, K * (O // ocn) // (O // ocn) * (O // ocn) // 128 * 128 // kcn * kcn)[:, :, :]) if False else             np.ascontiguousarray(a.transpose(2, 1, 0, 3).reshape(ocn, 128, -1))

    bqkv = np.asarray(inp["L0_attn_b_b_qkv"], f32)
    convw_raw = np.asarray(inp["L0_attn_a_conv_w"], f32)
    convw = np.zeros((128, 64), f32)
    for kc in range(16):
        for k in range(4):
            convw[:, kc * 4 + k] = convw_raw[kc * 128:(kc + 1) * 128, 0, k]

    def cols(v):
        return np.ascontiguousarray(np.asarray(v, f32).reshape(16, 128).T)

    def col1(v):
        return np.ascontiguousarray(np.asarray(v, f32).reshape(128, 1))

    w2T = tr(inp["L0_ffn_W2"])
    SEL = np.zeros((16, 2048), f32)
    for h in range(16):
        SEL[h, h * 128:(h + 1) * 128] = 1.0
    CS = np.zeros((128, 512), f32)
    for h in range(16):
        CS[:, h * 16 + h] = 1.0 / 128.0
        CS[:, 256 + h * 16 + h] = -1.0 / 128.0
    LNC = np.zeros((128, 3), f32)
    LNC[:, 0] = -1.0 / 2048.0
    LNC[:, 1] = 1.0 / 2048.0
    LNC[:, 2] = 1.0

    wqkv_f = np.asarray(inp["L0_attn_b_W_qkv"], f32)
    shared = {
        "wqT": slabbed(tr(inp["L0_attn_a_Wq"]), 16),
        "wkT": slabbed(tr(inp["L0_attn_a_Wk"]), 16),
        "wvT": slabbed(tr(inp["L0_attn_a_Wv"]), 16),
        "woT": slabbed(tr(inp["L0_attn_a_Wo"]), 16),
        "igT": slabbed(tr(inp["L0_attn_a_ig_w"]), 16),
        "ogT": slabbed(tr(inp["L0_attn_a_og_w"]), 16),
        "gammaT": tr(inp["L0_attn_a_gamma_w"]),
        "wqkv_qk": slabbed(wqkv_f[:, :4096], 32),
        "wqkv_v": np.ascontiguousarray(
            wqkv_f[:, 4096:].reshape(16, 128, 8, 256).transpose(2, 1, 0, 3)
            .reshape(8, 128, 4096)),
        "wproj": slabbed(np.ascontiguousarray(wqkv_f[:, 0:0]) if False else
                         np.asarray(inp["L0_attn_b_W_proj"], f32), 16),
        "w1T": slabbed(tr(inp["L0_ffn_W1"]), 64),
        "w3T": slabbed(tr(inp["L0_ffn_W3"]), 64),
        "w2resh": np.ascontiguousarray(w2T[_perm_jnew()]),
        "convw": convw,
        "convb": cols(inp["L0_attn_a_conv_b"]),
        "igb": cols(inp["L0_attn_a_ig_b"]),
        "ogb": cols(inp["L0_attn_a_og_b"]),
        "gammab": np.ascontiguousarray(
            np.asarray(inp["L0_attn_a_gamma_b"], f32).reshape(16, 1)),
        "vng": col1(inp["L0_attn_a_vn_g"]), "vnb": col1(inp["L0_attn_a_vn_b"]),
        "mng": col1(inp["L0_attn_a_mn_g"]), "mnb": col1(inp["L0_attn_a_mn_b"]),
        "gng": cols(inp["L0_attn_a_gn_g"]), "gnb": cols(inp["L0_attn_a_gn_b"]),
        "bqkv_c": np.ascontiguousarray(bqkv[:4096].reshape(32, 128).T),
        "bqkv_r": np.ascontiguousarray(bqkv[4096:].reshape(1, 2048)),
        "bproj": cols(inp["L0_attn_b_b_proj"]),
        "H128": _had(128),
        "I2H64": np.kron(np.eye(2, dtype=f32), _had(64)).astype(f32),
        "SEL": SEL, "CS": CS, "LNC": LNC,
        "ones_r": np.ones((1, 128), f32),
    }
    maps = []
    for c in range(8):
        b, half = c // 2, c % 2
        xs = xg[b, half * 1024:(half + 1) * 1024].T
        halo = np.zeros((2048, 3), f32) if half == 0 else \
            np.ascontiguousarray(xg[b, 1021:1024].T)
        m = dict(shared)
        m["x"] = np.ascontiguousarray(np.concatenate([halo, xs], axis=1))
        m["mask"] = np.full((128, 16), float(half), f32)
        maps.append(m)
    return maps


def kernel(**inp):
    nc = build_program()
    in_maps = host_inputs(inp)
    res = run_bass_kernel_spmd(nc, in_maps, core_ids=list(range(8)))
    out = np.zeros((4, 2048, 2048), dtype=np.float32)
    for c in range(8):
        b, half = c // 2, c % 2
        out[b, half * 1024:(half + 1) * 1024, :] = res.results[c]["y"].T
    return out


def build_program():
    nc = bass.Bass("TRN2", target_bir_lowering=False)
    E = {}
    for name, shape in [
            ("x", [C, TH]), ("wqT", [16, 128, C]), ("wkT", [16, 128, C]),
            ("wvT", [16, 128, C]), ("woT", [16, 128, C]), ("igT", [16, 128, C]),
            ("ogT", [16, 128, C]), ("gammaT", [C, 16]),
            ("wqkv_qk", [32, 128, C]), ("wqkv_v", [8, 128, 4096]),
            ("wproj", [16, 128, C]), ("w1T", [64, 128, C]), ("w3T", [64, 128, C]),
            ("w2resh", [NH, C]),
            ("convw", [128, 64]), ("convb", [128, 16]), ("igb", [128, 16]),
            ("ogb", [128, 16]), ("gammab", [16, 1]), ("vng", [128, 1]),
            ("vnb", [128, 1]), ("mng", [128, 1]), ("mnb", [128, 1]),
            ("gng", [128, 16]), ("gnb", [128, 16]), ("bqkv_c", [128, 32]),
            ("bqkv_r", [1, 2048]), ("bproj", [128, 16]),
            ("H128", [128, 128]), ("I2H64", [128, 128]), ("SEL", [16, 2048]),
            ("CS", [128, 512]), ("LNC", [128, 3]), ("ones_r", [1, 128]),
            ("mask", [128, 16])]:
        E[name] = nc.dram_tensor(name, shape, F32, kind="ExternalInput").ap()
    y_out = nc.dram_tensor("y", [C, T], F32, kind="ExternalOutput").ap()
    xdram = nc.dram_tensor("xdram", [C, T], F32, kind="Internal").ap()
    w1bf = nc.dram_tensor("w1bf", [64, 128, C], BF16, kind="Internal").ap()
    w3bf = nc.dram_tensor("w3bf", [64, 128, C], BF16, kind="Internal").ap()
    w2xbf = nc.dram_tensor("w2xbf", [16, 128, 64 * 128], BF16, kind="Internal").ap()
    udram = nc.dram_tensor("udram", [NH, T], BF16, kind="Internal").ap()
    hdram = nc.dram_tensor("hdram", [NH, T], BF16, kind="Internal").ap()
    ag_sin = nc.dram_tensor("ag_sin", [1, 2048], F32, kind="Internal").ap()
    ag_sout = nc.dram_tensor("ag_sout", [2, 2048], F32, kind="Internal").ap()
    ag_kin = nc.dram_tensor("ag_kin", [4112, 1024], BF16, kind="Internal").ap()
    ag_kout = nc.dram_tensor("ag_kout", [8224, 1024], BF16, kind="Internal").ap()

    with ExitStack() as stack:
        cx = Ctx(nc, stack)
        sb = lambda n, sh, dt: stack.enter_context(nc.sbuf_tensor(n, sh, dt))
        BP = sb("BP", [128, 4 * BW], BF16)
        B = [BP.ap()[:, i * BW:(i + 1) * BW] for i in range(4)]
        SC1 = sb("SC1", [128, 1032], F32)
        SC2 = SC1
        SBF = sb("SBF", [128, 2 * 1032], BF16)
        WS = sb("WS", [128, 2 * 8192], BF16)
        CPa = sb("CPa", [16, 1024], F32)
        CPb = sb("CPb", [16, 1024], F32)
        CPc = sb("CPc", [16, 1024], F32)
        ACf = sb("ACf", [16, 1024], F32)
        CPd = SBF.ap()[0:16, 0:2048].bitcast(F32)
        ZC = SBF.ap()[0:16, 0:1024]
        PCb = sb("PCb", [16, 1024], BF16)
        R1 = sb("R1", [1, 1032], F32)
        R2 = sb("R2", [1, 1032], F32)
        R3 = SC1.ap()[0:1, 0:1032]
        RBF = sb("RBF", [1, 2 * 1040], BF16)
        H128b = sb("H128b", [128, 128], BF16)
        I2H64b = sb("I2H64b", [128, 128], BF16)
        SELb = sb("SELb", [16, 2048], BF16)
        CSb = sb("CSb", [128, 512], BF16)
        LNCb = sb("LNCb", [128, 3], BF16)
        ONESb = sb("ONESb", [1, 128], BF16)
        BQKVRb = None
        CW = {}
        for n, sh in [("convw", [128, 64]), ("convb", [128, 16]),
                      ("igb", [128, 16]), ("ogb", [128, 16]),
                      ("gammab", [16, 1]), ("vng", [128, 1]), ("vnb", [128, 1]),
                      ("mng", [128, 1]), ("mnb", [128, 1]), ("gng", [128, 16]),
                      ("gnb", [128, 16]), ("bqkv_c", [128, 32]),
                      ("bproj", [128, 16]), ("mask", [128, 16])]:
            CW[n] = sb("c_" + n, sh, F32)
        SCOL = sb("SCOL", [128, 16], F32)
        SIN = sb("SIN", [128, 16], F32)
        MHall = sb("MHall", [128, 16], F32)
        PS = stack.enter_context(nc.psum_tensor("PS", [128, 4096], F32))

        pe, ve, se, ge, sy = nc.tensor, nc.vector, nc.scalar, nc.gpsimd, nc.sync
        _sids = {}

        def _ens(name):
            _sids[name] = nc.enter_named_scope(name, False)[0]

        def _lns(name):
            nc.leave_named_scope(name, _sids.pop(name), False)
        ALL_E = [pe, ve, se, ge, sy]

        def pcol(c0, n):
            return PS.ap()[:, c0:c0 + n]

        def prow(c0, n):
            return PS.ap()[0:1, c0:c0 + n]

        def pcmp(c0, n):
            return PS.ap()[0:16, c0:c0 + n]

        def psb(b0, n=512):
            return PS.ap()[:, b0 * 512: b0 * 512 + n]

        def csm(h):  return CSb[:, h * 16: h * 16 + 16]
        def csmn(h): return CSb[:, 256 + h * 16: 256 + h * 16 + 16]
        def selh(h): return SELb[:, h * 128:(h + 1) * 128]

        def pfence():
            cx.fence(ALL_E, ["pe", "dve", "act"])

        def tap(name, ap, cols, dt=BF16):
            if name in FENCES:
                cx.fence(ALL_E, ["pe", "dve", "act", "hpe", "hst", "hxo"])
            if name not in DBG:
                return
            dd = nc.dram_tensor("dbg_" + name, [ap.shape[0], cols], dt,
                                kind="ExternalOutput").ap()
            cx.fence(ALL_E, ["pe", "dve", "act", "hpe", "hst", "hxo"])
            cx.inc(sy.dma_start(dd, ap), "dbg", 16)
            cx.wait(sy, "dbg")

        for n in CW:
            cx.inc(sy.dma_start(CW[n][:], E[n][:]), "ini", 16)
        for dst, srcn in [(H128b, "H128"), (I2H64b, "I2H64"), (SELb, "SEL"),
                          (CSb, "CS"), (LNCb, "LNC"), (ONESb, "ones_r")]:
            cx.inc(ge.dma_start(dst[:], E[srcn][:]), "inig", 16)
        cx.wait(ve, "inig")
        xv = E["x"].rearrange("(kc kp) t -> kc kp t", kp=128)
        for kc in range(KC):
            cx.inc(ge.dma_start(BP.ap()[:, kc * 1028: kc * 1028 + TH], xv[kc]),
                   "xcast", 16)

        XNH = lambda kc: BP.ap()[:, kc * 1028: kc * 1028 + TH]
        XN = lambda kc: BP.ap()[:, kc * 1028 + 3: kc * 1028 + 3 + T]

        def ln_rows(n, m_col, sq_col):
            cx.inc(se.copy(R1[0:1, 0:n], prow(m_col, n)), "act", 1)
            cx.inc(se.copy(R2[0:1, 0:n], prow(sq_col, n)), "act", 1)
            cx.wait(ve, "act")
            cx.inc(ve.tensor_mul(R3[0:1, 0:n], R1[0:1, 0:n], R1[0:1, 0:n]), "dve", 1)
            cx.wait(ve, "dve")
            cx.inc(ve.tensor_sub(R2[0:1, 0:n], R2[0:1, 0:n], R3[0:1, 0:n]), "dve", 1)
            cx.wait(se, "dve")
            cx.inc(se.activation(R2[0:1, 0:n], R2[0:1, 0:n], AF.Ln, bias=np.float32(EPS)), "act", 1)
            cx.wait(se, "act")
            cx.inc(se.activation(R2[0:1, 0:n], R2[0:1, 0:n], AF.Exp, scale=np.float32(-0.5)), "act", 1)
            cx.wait(ve, "act")
            cx.inc(ve.tensor_mul(R1[0:1, 0:n], R1[0:1, 0:n], R2[0:1, 0:n]), "dve", 1)
            cx.wait(ve, "dve")
            cx.inc(ve.tensor_copy(RBF.ap()[0:1, 0:n], R2[0:1, 0:n]), "dve", 1)
            cx.wait(ve, "dve")
            cx.inc(ve.tensor_copy(RBF.ap()[0:1, 1040:1040 + n], R1[0:1, 0:n]), "dve", 1)
            cx.wait(pe, "dve")

        def bcast_rows(n, a_col, b_col):
            for t0 in range(0, n, 512):
                tl = min(512, n - t0)
                mm = pe.matmul(pcol(a_col + t0, tl), ONESb[0:1, 0:128],
                               RBF.ap()[0:1, t0:t0 + tl],
                               start=True, stop=False, skip_group_check=True)
                mm = pe.matmul(pcol(b_col + t0, tl), ONESb[0:1, 0:128],
                               RBF.ap()[0:1, 1040 + t0:1040 + t0 + tl],
                               start=True, stop=False, skip_group_check=True)
            cx.inc(mm, "pe", 1)

        _ens("ln1")
        cx.fence([se, pe, ve], ["xcast", "inig", "ini", "dve"])
        chunks3 = [(0, 512), (512, 512), (1024, 3)]
        bs = cx.base("l1st")
        for kc in range(KC):
            sl = (kc % 2) * 1032
            if kc >= 2:
                cx.wait(se, "l1st", bs["l1st"] + kc - 1)
            cx.inc(se.activation(SBF.ap()[:, sl:sl + TH], XNH(kc), AF.Square),
                   "act", 1)
            cx.wait(pe, "act")
            for (t0, tl) in chunks3:
                mm = pe.matmul(prow(0 + t0, tl), LNCb[:, 0:1],
                               XNH(kc)[:, t0:t0 + tl],
                               start=(kc == 0), stop=False,
                               skip_group_check=True)
                mm = pe.matmul(prow(1536 + t0, tl), LNCb[:, 1:2],
                               SBF.ap()[:, sl + t0: sl + t0 + tl],
                               start=(kc == 0), stop=False,
                               skip_group_check=True)
            cx.inc(mm, "l1st", 1)
        cx.fence([se], ["l1st"])
        ln_rows(TH, m_col=0, sq_col=1536)
        bcast_rows(TH, a_col=0, b_col=2048)
        cx.wait(ve, "pe")
        for kc in range(KC):
            xb = XNH(kc)
            cx.inc(ve.tensor_tensor(xb, xb, pcol(0, TH), ALU.mult), "dve", 1)
            cx.wait(ve, "dve")
            cx.inc(ve.tensor_tensor(xb, xb, pcol(2048, TH), ALU.add), "dve", 1)
            cx.wait(ve, "dve")

        tap("xn", BP.ap()[:, 0:16 * 1028], 16 * 1028)
        _lns("ln1")
        _ens("conv")
        for kc in range(KC):
            acc = SC1.ap()[:, 0:T]
            cx.wait(ve, "act")
            cx.inc(ve.tensor_scalar(acc, BP.ap()[:, kc * 1028: kc * 1028 + T],
                                    CW["convw"][:, kc * 4: kc * 4 + 1], None,
                                    ALU.mult), "dve", 1)
            for k in range(1, 4):
                cx.wait(ve, "dve")
                cx.inc(ve.scalar_tensor_tensor(
                    acc, BP.ap()[:, kc * 1028 + k: kc * 1028 + k + T],
                    CW["convw"][:, kc * 4 + k: kc * 4 + k + 1], acc,
                    ALU.mult, ALU.add), "dve", 1)
            cx.wait(se, "dve")
            cx.inc(se.activation(B[1][:, kc * T:(kc + 1) * T], acc, AF.Silu,
                                 bias=CW["convb"][:, kc:kc + 1]), "act", 1)
        XC = lambda kc: B[1][:, kc * T:(kc + 1) * T]

        _lns("conv")
        _ens("gamma")
        cx.wait(ge, "act")
        cx.inc(ge.dma_start(WS[:, 0:256],
                            E["gammaT"].rearrange("(kc kp) h -> kp kc h", kp=128)),
               "gw", 16)
        cx.fence([pe], ["gw", "act", "dve"])
        for kc in range(KC):
            for i in range(2):
                mm = pe.matmul(pcmp(i * 512, 512), WS[:, kc * 16: kc * 16 + 16],
                               XC(kc)[:, i * 512: i * 512 + 512],
                               start=(kc == 0), stop=False, skip_group_check=True)
        cx.inc(mm, "pe", 1)
        cx.wait(se, "pe")
        cx.inc(se.activation(ACf[:], pcmp(0, 1024), AF.Sigmoid,
                             bias=CW["gammab"][:]), "act", 1)
        cx.wait(ve, "act")
        cx.inc(ve.memset(ZC[:], 0.0), "dve", 1)
        cx.wait(ve, "dve")
        cx.inc(ve.tensor_tensor_scan(PCb[:], ACf[:], ZC[:], 1,
                                     ALU.mult, ALU.add), "dve", 1)

        tap("xc", B[1][:, 0:16 * T], 16 * T)
        tap("a", ACf[:], 1024, F32)
        tap("p", PCb[:], 1024, BF16)
        _lns("gamma")
        def head_proj(tag, wsrc, dst, want_mean, want_sq, act_fn=AF.Copy,
                      bias=None, rhs=None):
            if rhs is None:
                rhs = XN
            wv = wsrc
            b0 = cx.base("hw0", "hw1", "hpe", "hst")
            info = []
            actc = []
            for h in range(17):
                if h < 16:
                    buf = h % 2
                    if h >= 2:
                        cx.wait(ge, "hpe", b0["hpe"] + h - 1)
                    cx.inc(ge.dma_start(WS[:, buf * 8192: buf * 8192 + 2048],
                                        wv[h]), f"hw{buf}", 16)
                    cx.wait(pe, f"hw{buf}", cx.val(f"hw{buf}"))
                    if h >= 2:
                        cx.wait(pe, "act", actc[h - 2])
                    pb = (h % 2) * 2
                    for kc in range(KC):
                        lhsT = WS[:, buf * 8192 + kc * 128: buf * 8192 + kc * 128 + 128]
                        for i in range(2):
                            mm = pe.matmul(psb(pb + i), lhsT,
                                           rhs(kc)[:, i * 512: i * 512 + 512],
                                           start=(kc == 0), stop=False,
                                           skip_group_check=True)
                    cx.inc(mm, "hpe", 1)
                    cx.wait(se, "hpe", b0["hpe"] + h + 1)
                    if h >= 2:
                        cx.wait(se, "hst", b0["hst"] + h - 1)
                    sl = buf * 1032
                    kw = {}
                    if bias is not None:
                        kw["bias"] = bias[:, h:h + 1]
                    i1 = se.activation(dst(h), psb(pb, 1024), act_fn, **kw)
                    cx.inc(i1, "act", 1)
                    if want_sq:
                        i2 = se.activation(SBF.ap()[:, sl:sl + 1024], psb(pb, 1024),
                                           AF.Square)
                        cx.inc(i2, "act", 1)
                    actc.append(cx.val("act"))
                    info.append(sl)
                hs = h - 1
                if hs >= 0:
                    sl2 = info[hs]
                    cx.wait(pe, "act", actc[hs])
                    if want_mean:
                        for i in range(2):
                            mm = pe.matmul(pcmp(2048 + i * 512, 512), csmn(hs),
                                           dst(hs)[:, i * 512: i * 512 + 512],
                                           start=(hs == 0), stop=False,
                                           skip_group_check=True)
                    if want_sq:
                        csx = csm(hs)
                        for i in range(2):
                            mm = pe.matmul(pcmp(3072 + i * 512, 512), csx,
                                           SBF.ap()[:, sl2 + i * 512: sl2 + i * 512 + 512],
                                           start=(hs == 0), stop=False,
                                           skip_group_check=True)
                    cx.inc(mm, "hst", 1)
            cx.wait(se, "hst", b0["hst"] + 16)
            if want_mean:
                cx.inc(se.copy(CPa[:], pcmp(2048, 1024)), "act", 1)
            if want_sq:
                cx.inc(se.copy(CPb[:], pcmp(3072, 1024)), "act", 1)
            cx.fence([ve, pe], ["act"])

        def compact_rstd(dst, tmp, ln_style):
            if ln_style:
                cx.inc(ve.tensor_mul(tmp[:], CPa[:], CPa[:]), "dve", 1)
                cx.wait(ve, "dve")
                cx.inc(ve.tensor_sub(tmp[:], CPb[:], tmp[:]), "dve", 1)
                cx.wait(se, "dve")
                cx.inc(se.activation(tmp[:], tmp[:], AF.Ln, bias=np.float32(EPS)), "act", 1)
            else:
                cx.inc(ve.tensor_scalar_max(tmp[:], CPb[:], 7.8125e-27), "dve", 1)
                cx.wait(se, "dve")
                cx.inc(se.activation(tmp[:], tmp[:], AF.Ln, scale=np.float32(128.0),
                                     bias=np.float32(1e-38)), "act", 1)
            cx.wait(se, "act")
            cx.inc(se.activation(dst[:], tmp[:], AF.Exp, scale=np.float32(-0.5)), "act", 1)
            cx.fence([ve, pe], ["act"])

        def apply_affine(dst_fn, alpha_cp, beta_cp, g_col_fn, b_col_fn,
                         extra_mul_fn=None):
            ab_bf = SBF.ap()[0:16, 0:1024]
            bb_bf = SBF.ap()[0:16, 1032:1032 + 1024]
            cx.inc(ve.tensor_copy(ab_bf, alpha_cp[:]), "dve", 1)
            cx.wait(ve, "dve")
            if beta_cp is not None:
                cx.inc(ve.tensor_copy(bb_bf, beta_cp[:]), "dve", 1)
                cx.wait(ve, "dve")
            dvh = []
            for h in range(16):
                bo = (h % 2) * 2048
                if h >= 2:
                    cx.wait(pe, "dve", dvh[h - 2])
                else:
                    cx.fence([pe], ["dve"])
                for i in range(2):
                    mm = pe.matmul(pcol(bo + i * 512, 512), selh(h),
                                   ab_bf[:, i * 512: i * 512 + 512],
                                   start=True, stop=False, skip_group_check=True)
                    if beta_cp is not None:
                        mm = pe.matmul(pcol(bo + 1024 + i * 512, 512), selh(h),
                                       bb_bf[:, i * 512: i * 512 + 512],
                                       start=True, stop=False,
                                       skip_group_check=True)
                cx.inc(mm, "pe", 1)
                cx.wait(ve, "pe")
                d = dst_fn(h)
                cx.inc(ve.tensor_tensor(d, d, pcol(bo, 1024), ALU.mult), "dve", 1)
                cx.wait(ve, "dve")
                if beta_cp is not None:
                    cx.inc(ve.tensor_tensor(d, d, pcol(bo + 1024, 1024), ALU.add),
                           "dve", 1)
                    cx.wait(ve, "dve")
                if g_col_fn is not None:
                    cx.inc(ve.tensor_scalar(d, d, g_col_fn(h), b_col_fn(h),
                                            ALU.mult, ALU.add), "dve", 1)
                    cx.wait(ve, "dve")
                if extra_mul_fn is not None:
                    cx.inc(ve.tensor_tensor(d, d, extra_mul_fn(h), ALU.mult),
                           "dve", 1)
                    cx.wait(ve, "dve")
                dvh.append(cx.val("dve"))

        def gate_proj(tag, wsrc, rhs_fn, out_fn, bias_cols):
            wv = wsrc
            b0 = cx.base("hw0", "hw1", "hpe")
            actc = []
            dvec = []
            for h in range(16):
                buf = h % 2
                if h >= 2:
                    cx.wait(ge, "hpe", b0["hpe"] + h - 1)
                cx.inc(ge.dma_start(WS[:, buf * 8192: buf * 8192 + 2048],
                                    wv[h]), f"hw{buf}", 16)
                cx.wait(pe, f"hw{buf}", cx.val(f"hw{buf}"))
                if h >= 2:
                    cx.wait(pe, "act", actc[h - 2])
                pb = (h % 2) * 2
                for kc in range(KC):
                    lhsT = WS[:, buf * 8192 + kc * 128: buf * 8192 + kc * 128 + 128]
                    for i in range(2):
                        mm = pe.matmul(psb(pb + i), lhsT,
                                       rhs_fn(kc)[:, i * 512: i * 512 + 512],
                                       start=(kc == 0), stop=False,
                                       skip_group_check=True)
                cx.inc(mm, "hpe", 1)
                cx.wait(se, "hpe", b0["hpe"] + h + 1)
                if h >= 2:
                    cx.wait(se, "dve", dvec[h - 2])
                sl = buf * 1032
                i1 = se.activation(SBF.ap()[:, sl:sl + 1024], psb(pb, 1024),
                                   AF.Sigmoid, bias=bias_cols[:, h:h + 1])
                cx.inc(i1, "act", 1)
                actc.append(cx.val("act"))
                cx.wait(ve, "act")
                tt = out_fn(h, SBF.ap()[:, sl:sl + 1024])
                cx.inc(tt, "dve", 1)
                dvec.append(cx.val("dve"))
            cx.fence([pe, ve, se, ge], ["dve", "act"])

        V = lambda h: B[2][:, h * T:(h + 1) * T]
        pfence()
        with nc.named_scope("vproj"):
            head_proj("v", E["wvT"], V, want_mean=True, want_sq=True)
        compact_rstd(CPc, CPd, ln_style=True)
        cx.inc(ve.tensor_mul(CPa[:], CPa[:], CPc[:]), "dve", 1)
        cx.wait(ve, "dve")
        with nc.named_scope("vapply"):
            apply_affine(V, CPc, CPa, lambda h: CW["vng"][:, 0:1],
                         lambda h: CW["vnb"][:, 0:1])

        tap("vn", B[2][:, 0:16 * T], 16 * T)
        K_ = lambda h: B[3][:, h * T:(h + 1) * T]
        pfence()
        with nc.named_scope("kproj"):
            head_proj("k", E["wkT"], K_, want_mean=False, want_sq=True)
        compact_rstd(CPb, CPd, ln_style=False)
        apply_affine(K_, CPb, None, None, None)
        cx.inc(ve.tensor_mul(B[2][:, 0:16 * T], B[2][:, 0:16 * T],
                             B[3][:, 0:16 * T]), "dve", 1)
        cx.wait(ve, "dve")

        tap("kv", B[2][:, 0:16 * T], 16 * T)
        pfence()
        with nc.named_scope("ig"):
            gate_proj("ig", E["igT"], XC,
                      lambda h, sig: ve.tensor_tensor(K_(h), sig,
                                                      B[2][:, h * T:(h + 1) * T],
                                                      ALU.mult),
                      CW["igb"])
        BT = K_

        tap("b", B[3][:, 0:16 * T], 16 * T)
        MEM = lambda h: B[2][:, h * T:(h + 1) * T]
        pfence()

        def bcast_scan(src_cp, op_fn):
            b0 = cx.base("hpe")
            dvec = []
            for h in range(16):
                pb = (h % 2) * 2
                if h >= 2:
                    cx.wait(pe, "dve", dvec[h - 2])
                for i in range(2):
                    mm = pe.matmul(psb(pb + i), selh(h),
                                   src_cp[:, i * 512: i * 512 + 512],
                                   start=True, stop=False, skip_group_check=True)
                cx.inc(mm, "hpe", 1)
                cx.wait(ve, "hpe", b0["hpe"] + h + 1)
                op = op_fn(h, psb(pb, 1024))
                cx.inc(op, "dve", 1)
                dvec.append(cx.val("dve"))
            cx.fence([pe, ve], ["dve"])

        _ens("scan")
        b0s = cx.base("hpe", "hsl0", "hsl1")
        cx.fence([sy], ["dve", "act"])
        dvec = []
        for h in range(16):
            pb = (h % 2) * 2
            c0 = (h % 2) * 512
            if h >= 2:
                cx.wait(sy, "hpe", b0s["hpe"] + h - 1)
            cx.inc(sy.dma_start(SC1.ap()[0:16, c0:c0 + 128],
                                E["SEL"][:, h * 128:(h + 1) * 128]), f"hsl{h % 2}", 16)
            cx.wait(pe, f"hsl{h % 2}", cx.val(f"hsl{h % 2}"))
            if h >= 2:
                cx.wait(pe, "dve", dvec[h - 2])
            for i in range(2):
                mm = pe.matmul(psb(pb + i), SC1.ap()[0:16, c0:c0 + 128],
                               ACf[:, i * 512: i * 512 + 512],
                               start=True, stop=False, skip_group_check=True)
            cx.inc(mm, "hpe", 1)
            cx.wait(ve, "hpe", b0s["hpe"] + h + 1)
            op = ve.tensor_tensor_scan(MEM(h), psb(pb, 1024), BT(h), 0,
                                       ALU.mult, ALU.add)
            cx.inc(op, "dve", 1)
            dvec.append(cx.val("dve"))
        cx.fence([pe, ve, sy], ["dve"])
        _lns("scan")
        _ens("relay")
        ext = B[2][:, 0:16 * T].rearrange("p (h t) -> p h t", t=T)[:, :, T - 1:T]
        cx.inc(ve.tensor_copy(SCOL.ap().rearrange("p (h o) -> p h o", o=1), ext),
               "dve", 1)
        cx.wait(sy, "dve")
        cx.inc(sy.dma_start(ag_sin.rearrange("one (p h) -> (one p) h", p=128),
                            SCOL[:]), "ag1", 16)
        cx.wait(ge, "ag1")
        cx.inc(ge.collective_compute(
            "AllGather", ALU.bypass, ins=[ag_sin.opt()], outs=[ag_sout.opt()],
            replica_groups=[[0, 1], [2, 3], [4, 5], [6, 7]]), "cc", 1)
        cx.wait(sy, "cc")
        cx.inc(sy.dma_start(SIN[:],
                            ag_sout[0:1, :].rearrange("one (p h) -> (one p) h", p=128)),
               "ag2", 16)
        cx.wait(ve, "ag2")
        cx.inc(ve.tensor_mul(SIN[:], SIN[:], CW["mask"][:]), "dve", 1)
        cx.wait(ve, "dve")
        bcast_scan(PCb, lambda h, pb_: ve.scalar_tensor_tensor(
            MEM(h), pb_, SIN[:, h:h + 1], MEM(h), ALU.mult, ALU.add))
        _lns("relay")

        tap("mem", B[2][:, 0:16 * T], 16 * T)
        def sbuf_stats(src_fn, want_mean=True):
            b0 = cx.base("hst")
            for h in range(16):
                sl = (h % 2) * 1032
                if h >= 2:
                    cx.wait(se, "hst", b0["hst"] + h - 1)
                cx.inc(se.activation(SBF.ap()[:, sl:sl + 1024], src_fn(h),
                                     AF.Square), "act", 1)
                cx.wait(pe, "act")
                for i in range(2):
                    if want_mean:
                        mm = pe.matmul(pcmp(2048 + i * 512, 512), csmn(h),
                                       src_fn(h)[:, i * 512: i * 512 + 512],
                                       start=(h == 0), stop=False,
                                       skip_group_check=True)
                    mm = pe.matmul(pcmp(3072 + i * 512, 512),
                                   csm(h),
                                   SBF.ap()[:, sl + i * 512: sl + i * 512 + 512],
                                   start=(h == 0), stop=False,
                                   skip_group_check=True)
                cx.inc(mm, "hst", 1)
            cx.wait(se, "hst", b0["hst"] + 16)
            if want_mean:
                cx.inc(se.copy(CPa[:], pcmp(2048, 1024)), "act", 1)
            cx.inc(se.copy(CPb[:], pcmp(3072, 1024)), "act", 1)
            cx.fence([ve, pe], ["act"])

        pfence()
        _ens("mnstat")
        sbuf_stats(MEM, want_mean=True)
        compact_rstd(CPc, CPd, ln_style=True)
        cx.inc(ve.tensor_mul(CPa[:], CPa[:], CPc[:]), "dve", 1)
        cx.wait(ve, "dve")
        _lns("mnstat")

        Q_ = lambda h: B[3][:, h * T:(h + 1) * T]
        pfence()
        with nc.named_scope("qproj"):
            head_proj("q", E["wqT"], Q_, want_mean=False, want_sq=True)
        compact_rstd(CPb, CPd, ln_style=False)
        apply_affine(Q_, CPb, None, None, None)

        tap("qn", B[3][:, 0:16 * T], 16 * T)
        _ens("outapply")
        OUT = lambda h: B[0][:, h * T:(h + 1) * T]
        for h in range(16):
            cx.inc(ve.tensor_copy(OUT(h), MEM(h)), "dve", 1)
            cx.wait(ve, "dve")
        apply_affine(OUT, CPc, CPa, lambda h: CW["mng"][:, 0:1],
                     lambda h: CW["mnb"][:, 0:1], extra_mul_fn=Q_)

        _lns("outapply")
        _ens("gn")
        pfence()
        sbuf_stats(OUT, want_mean=True)
        compact_rstd(CPc, CPd, ln_style=True)
        cx.inc(ve.tensor_mul(CPa[:], CPa[:], CPc[:]), "dve", 1)
        cx.wait(ve, "dve")
        apply_affine(OUT, CPc, CPa, lambda h: CW["gng"][:, h:h + 1],
                     lambda h: CW["gnb"][:, h:h + 1])

        tap("out_gn", B[0][:, 0:16 * T], 16 * T)
        _lns("gn")
        pfence()
        with nc.named_scope("og"):
            gate_proj("og", E["ogT"], XC,
                      lambda h, sig: ve.tensor_tensor(OUT(h), OUT(h), sig, ALU.mult),
                      CW["ogb"])

        def residual_proj(tag, wsrc, rhs_fn, res_fn, dst_dram,
                          bias_cols=None, next_ln=True):
            wv = wsrc
            dstv = dst_dram.rearrange("(oc op) t -> oc op t", op=128)
            b0 = cx.base("hw0", "hw1", "hpe", "hst", "hxi", "hxo")
            dvec = []
            for oc in range(16):
                buf = oc % 2
                if oc >= 2:
                    cx.wait(ge, "hpe", b0["hpe"] + oc - 1)
                cx.inc(ge.dma_start(WS[:, buf * 8192: buf * 8192 + 2048],
                                    wv[oc]), f"hw{buf}", 16)
                cx.wait(pe, f"hw{buf}", cx.val(f"hw{buf}"))
                if oc >= 2:
                    cx.wait(pe, "dve", dvec[oc - 2])
                pb = (oc % 2) * 2
                for kc in range(KC):
                    lhsT = WS[:, buf * 8192 + kc * 128: buf * 8192 + kc * 128 + 128]
                    for i in range(2):
                        mm = pe.matmul(psb(pb + i), lhsT,
                                       rhs_fn(kc)[:, i * 512: i * 512 + 512],
                                       start=(kc == 0), stop=False,
                                       skip_group_check=True)
                cx.inc(mm, "hpe", 1)
                cx.wait(sy, "dve")
                cx.wait(sy, "act")
                if oc >= 1:
                    cx.wait(sy, "hxo", b0["hxo"] + 16 * oc)
                cx.inc(sy.dma_start(SC1.ap()[:, 0:T], res_fn(oc)), "hxi", 16)
                cx.wait(ve, "hxi")
                cx.wait(ve, "hpe", b0["hpe"] + oc + 1)
                cx.wait(ve, "hxo", b0["hxo"] + 16 * oc)
                cx.wait(ve, "act")
                tt_ins = ve.tensor_tensor(SC2.ap()[:, 0:T], psb(pb, 1024),
                                          SC1.ap()[:, 0:T], ALU.add)
                cx.inc(tt_ins, "dve", 1)
                dvec.append(cx.val("dve"))
                cx.wait(ve, "dve")
                if bias_cols is not None:
                    cx.inc(ve.tensor_scalar(SC2.ap()[:, 0:T], SC2.ap()[:, 0:T],
                                            bias_cols[:, oc:oc + 1], None,
                                            ALU.add), "dve", 1)
                    cx.wait(ve, "dve")
                cx.wait(sy, "dve")
                cx.inc(sy.dma_start(dstv[oc], SC2.ap()[:, 0:T]), "hxo", 16)
                if next_ln:
                    cx.wait(se, "dve")
                    cx.wait(se, "hst", b0["hst"] + oc)
                    cx.inc(se.activation(SBF.ap()[:, 0:1024], SC2.ap()[:, 0:T],
                                         AF.Square), "act", 1)
                    cx.inc(se.copy(SBF.ap()[:, 1032:1032 + 1024], SC2.ap()[:, 0:T]),
                           "act", 1)
                    cx.wait(pe, "act")
                    for i in range(2):
                        mm = pe.matmul(prow(2048 + i * 512, 512), LNCb[:, 0:1],
                                       SBF.ap()[:, 1032 + i * 512:1032 + i * 512 + 512],
                                       start=(oc == 0), stop=False,
                                       skip_group_check=True)
                        mm = pe.matmul(prow(3072 + i * 512, 512), LNCb[:, 1:2],
                                       SBF.ap()[:, i * 512: i * 512 + 512],
                                       start=(oc == 0), stop=False,
                                       skip_group_check=True)
                    cx.inc(mm, "hst", 1)
            cx.fence(ALL_E, ["hxo", "hst", "dve", "act"])

        xin_res = lambda oc: E["x"].rearrange("(oc op) t -> oc op t", op=128)[oc][:, 3:3 + T]
        with nc.named_scope("wo"):
            residual_proj("wo", E["woT"], OUT, xin_res, xdram, next_ln=True)

        def finalize_trunk():
            xdv = xdram.rearrange("(oc op) t -> oc op t", op=128)
            ydv = y_out.rearrange("(oc op) t -> oc op t", op=128)
            for oc in range(16):
                cx.inc(sy.dma_start(SC1.ap()[:, 0:T], xdv[oc]), "fcp", 16)
                cx.wait(sy, "fcp")
                cx.inc(sy.dma_start(ydv[oc], SC1.ap()[:, 0:T]), "fcp", 16)
                cx.wait(sy, "fcp")

        if PHASE_LIMIT == 1:
            finalize_trunk()
            return nc

        def ln_apply(dst_b):
            _ens("ln_apply")
            ln_rows(T, m_col=2048, sq_col=3072)
            bcast_rows(T, a_col=0, b_col=1024)
            xdv = xdram.rearrange("(kc kp) t -> kc kp t", kp=128)
            b0 = cx.base("xl0", "xl1", "pe")
            dvec = []
            for kc in range(KC):
                buf = kc % 2
                if kc >= 2:
                    cx.wait(ge, "dve", dvec[kc - 2])
                cx.inc(ge.dma_start(dst_b[:, kc * T:(kc + 1) * T], xdv[kc]),
                       f"xl{buf}", 16)
                cx.wait(ve, f"xl{buf}", cx.val(f"xl{buf}"))
                cx.wait(ve, "pe")
                cx.inc(ve.tensor_tensor(dst_b[:, kc * T:(kc + 1) * T],
                                        dst_b[:, kc * T:(kc + 1) * T],
                                        pcol(0, 1024), ALU.mult), "dve", 1)
                cx.wait(ve, "dve")
                cx.inc(ve.tensor_tensor(dst_b[:, kc * T:(kc + 1) * T],
                                        dst_b[:, kc * T:(kc + 1) * T],
                                        pcol(1024, 1024), ALU.add), "dve", 1)
                dvec.append(cx.val("dve"))
                cx.wait(ve, "dve")
            _lns("ln_apply")

        def w2x_prep():
            _ens("w2xprep")
            dstv = None
            cx.fence([pe], ["dve", "act"])
            b0 = cx.base("hw0", "hw1", "hpe")
            actc = []
            stc = []
            for G in range(64):
                buf = G % 2
                if G >= 2:
                    cx.wait(ge, "hpe", b0["hpe"] + G - 1)
                cx.inc(ge.dma_start(WS[:, buf * 8192: buf * 8192 + 2048],
                                    E["w2resh"][G * 128:(G + 1) * 128, :]),
                       f"hw{buf}", 16)
                cx.wait(pe, f"hw{buf}", cx.val(f"hw{buf}"))
                if G >= 2:
                    cx.wait(pe, "act", actc[G - 2])
                pb = (G % 2) * 4
                for i in range(4):
                    mm = pe.matmul(psb(pb + i), I2H64b[:],
                                   WS[:, buf * 8192 + i * 512: buf * 8192 + i * 512 + 512],
                                   start=True, stop=False, skip_group_check=True)
                cx.inc(mm, "hpe", 1)
                slot = G % 8
                sb_ = B[1][:, slot * 2048: slot * 2048 + 2048]
                cx.wait(se, "hpe", b0["hpe"] + G + 1)
                if G >= 8:
                    cx.wait(se, f"st{slot}", stc[G - 8])
                cx.inc(se.activation(sb_[:, 0:1024], psb(pb, 1024),
                                     AF.Identity, scale=np.float32(SCL)), "act", 1)
                i2 = se.activation(sb_[:, 1024:2048], psb(pb + 2, 1024),
                                   AF.Identity, scale=np.float32(SCL))
                cx.inc(i2, "act", 1)
                actc.append(cx.val("act"))
                cx.wait(sy, "act")
                dG = w2xbf[:, 2 * G:2 * G + 2, :].rearrange(
                    "oc dlow (kc ow) -> dlow kc oc ow", ow=128)
                cx.inc(sy.dma_start(dG, sb_.rearrange(
                    "p (oc ow) -> p oc ow", ow=128) if False else sb_),
                       f"st{slot}", 16)
                stc.append(cx.val(f"st{slot}"))
            cx.fence(ALL_E, [f"st{i}" for i in range(8)] + ["act"])
            _lns("w2xprep")

        HID = lambda jc: BP.ap()[:, BW + jc * 512: BW + jc * 512 + 512]

        def ffn_eval(tag, xn_b, first_eval, dst_dram, next_ln):
            _ens(tag)
            xn_fn = lambda kc: xn_b[:, kc * T:(kc + 1) * T]
            cx.fence([pe], ["dve", "act"])
            def _fs(nm):
                if FINE:
                    _ens(tag + nm)
            def _fe(nm):
                if FINE:
                    _lns(tag + nm)
            uv = udram.rearrange("(jc kp) t -> jc kp t", kp=128)
            hv = hdram.rearrange("(jc kp) t -> jc kp t", kp=128)
            for ip, (wname, wfp, wbf) in enumerate(
                    (("w1", E["w1T"], w1bf), ("w3", E["w3T"], w3bf))):
                wv = wfp
                wvb = wbf
                _fs("p%d" % (ip + 1))
                ldq = ge if first_eval else sy
                auxq = sy if first_eval else ge
                lds = "hw" if first_eval else "sw"
                fsp = "sf" if first_eval else "gf"
                b0 = cx.base("hpe", "htee0", "htee1")
                stc = {}
                ulc = {}
                actc = []
                for jc in range(64):
                    buf = jc % 2
                    slot = jc % 4
                    if jc >= 2:
                        cx.wait(ldq, "hpe", b0["hpe"] + jc - 1)
                        if first_eval:
                            cx.wait(ldq, f"htee{buf}", cx.val(f"htee{buf}"))
                    if first_eval:
                        cx.inc(ge.dma_start(WS[:, buf * 8192: buf * 8192 + 2048],
                                            wv[jc]), f"hw{buf}", 16)
                        cx.wait(sy, f"hw{buf}", cx.val(f"hw{buf}"))
                        cx.inc(sy.dma_start(wvb[jc],
                                            WS[:, buf * 8192: buf * 8192 + 2048]),
                               f"htee{buf}", 16)
                    else:
                        cx.inc(sy.dma_start(WS[:, buf * 8192: buf * 8192 + 2048],
                                            wvb[jc]), f"sw{buf}", 16)
                    cx.wait(pe, f"{lds}{buf}", cx.val(f"{lds}{buf}"))
                    pb = (jc % 2) * 2
                    if jc >= 2:
                        cx.wait(pe, "act" if ip == 0 else "dve", actc[jc - 2])
                    for kc in range(KC):
                        lhsT = WS[:, buf * 8192 + kc * 128: buf * 8192 + kc * 128 + 128]
                        for i in range(2):
                            mm = pe.matmul(psb(pb + i), lhsT,
                                           xn_fn(kc)[:, i * 512: i * 512 + 512],
                                           start=(kc == 0), stop=False,
                                           skip_group_check=True)
                    cx.inc(mm, "hpe", 1)
                    if ip == 0:
                        sb_ = B[3][:, slot * 1024: slot * 1024 + 1024]
                        cx.wait(se, "hpe", b0["hpe"] + jc + 1)
                        if jc >= 4:
                            cx.wait(se, f"{fsp}s{slot}", stc[jc - 4])
                        i1 = se.activation(sb_, psb(pb, 1024), AF.Silu)
                        cx.inc(i1, "act", 1)
                        actc.append(cx.val("act"))
                        cx.wait(auxq, "act")
                        cx.inc(auxq.dma_start(uv[jc], sb_), f"{fsp}s{slot}", 16)
                        stc[jc] = cx.val(f"{fsp}s{slot}")
                    else:
                        us_ = B[3][:, (4 + slot) * 1024: (4 + slot) * 1024 + 1024]
                        sb_ = B[3][:, slot * 1024: slot * 1024 + 1024]
                        if jc >= 4:
                            cx.wait(auxq, "dve", ulc[jc - 4])
                        cx.inc(auxq.dma_start(us_, uv[jc]), f"{fsp}u{slot}", 16)
                        cx.wait(ve, f"{fsp}u{slot}", cx.val(f"{fsp}u{slot}"))
                        cx.wait(ve, "hpe", b0["hpe"] + jc + 1)
                        if jc >= 4:
                            cx.wait(ve, f"{fsp}s{slot}", stc[jc - 4])
                        i1 = ve.tensor_tensor(sb_, psb(pb, 1024), us_, ALU.mult)
                        cx.inc(i1, "dve", 1)
                        actc.append(cx.val("dve"))
                        ulc[jc] = cx.val("dve")
                        cx.wait(auxq, "dve")
                        cx.inc(auxq.dma_start(hv[jc], sb_), f"{fsp}s{slot}", 16)
                        stc[jc] = cx.val(f"{fsp}s{slot}")
                cx.fence(ALL_E, ["dve", "act", "htee0", "htee1"]
                         + [f"{fsp}s{i}" for i in range(4)])
                _fe("p%d" % (ip + 1))
            for tt in range(2):
                _fs("p3%s" % "ab"[tt])
                t0 = tt * 512
                cx.inc(ge.dma_start(
                    BP.ap()[:, BW:BW + 64 * 512].rearrange("p (jc q) -> p jc q", jc=64),
                    hdram.rearrange("(jc kp) t -> kp jc t", kp=128)[:, :, t0:t0 + 512]),
                    "hh", 16)
                cx.fence([pe], ["hh"])
                b0 = cx.base("hpe")
                actc = []
                for c_ in range(64):
                    pb = c_ % 4
                    if c_ >= 4:
                        cx.wait(pe, "act", actc[c_ - 4])
                    mm = pe.matmul(psb(pb), H128b[:], HID(c_),
                                   start=True, stop=False, skip_group_check=True)
                    cx.inc(mm, "hpe", 1)
                    cx.wait(se, "hpe", b0["hpe"] + c_ + 1)
                    i1 = se.copy(HID(c_), psb(pb))
                    cx.inc(i1, "act", 1)
                    actc.append(cx.val("act"))
                cx.fence([pe, ve, se], ["act"])
                wv2 = w2xbf
                xdv = xdram.rearrange("(oc op) t -> oc op t", op=128)
                dstv = dst_dram.rearrange("(oc op) t -> oc op t", op=128)
                b0 = cx.base("sw0", "sw1", "hpe", "hst", "hxi", "hxo")
                dvec = []
                for oc in range(16):
                    buf = oc % 2
                    if oc >= 2:
                        cx.wait(sy, "hpe", b0["hpe"] + oc - 1)
                    cx.inc(sy.dma_start(WS[:, buf * 8192: buf * 8192 + 8192],
                                        wv2[oc]), f"sw{buf}", 16)
                    cx.wait(pe, f"sw{buf}", cx.val(f"sw{buf}"))
                    pb = oc % 4
                    if oc >= 4:
                        cx.wait(pe, "dve", dvec[oc - 4])
                    for kc in range(64):
                        lhsT = WS[:, buf * 8192 + kc * 128: buf * 8192 + kc * 128 + 128]
                        mm = pe.matmul(psb(pb), lhsT, HID(kc),
                                       start=(kc == 0), stop=False,
                                       skip_group_check=True)
                    cx.inc(mm, "hpe", 1)
                    cx.wait(sy, "dve")
                    cx.wait(sy, "act")
                    if oc >= 1:
                        cx.wait(sy, "hxo", b0["hxo"] + 16 * oc)
                    cx.inc(sy.dma_start(SC1.ap()[:, 0:512], xdv[oc][:, t0:t0 + 512]),
                           "hxi", 16)
                    cx.wait(ve, "hxi")
                    cx.wait(ve, "hpe", b0["hpe"] + oc + 1)
                    cx.wait(ve, "hxo", b0["hxo"] + 16 * oc)
                    cx.wait(ve, "act")
                    tt_ins = ve.tensor_tensor(SC2.ap()[:, 0:512], psb(pb),
                                              SC1.ap()[:, 0:512], ALU.add)
                    cx.inc(tt_ins, "dve", 1)
                    dvec.append(cx.val("dve"))
                    cx.wait(sy, "dve")
                    cx.inc(sy.dma_start(dstv[oc][:, t0:t0 + 512], SC2.ap()[:, 0:512]),
                           "hxo", 16)
                    if next_ln:
                        cx.wait(se, "dve")
                        cx.wait(se, "hst", b0["hst"] + oc)
                        cx.inc(se.activation(SBF.ap()[:, 0:512], SC2.ap()[:, 0:512],
                                             AF.Square), "act", 1)
                        cx.inc(se.copy(SBF.ap()[:, 1032:1032 + 512],
                                       SC2.ap()[:, 0:512]), "act", 1)
                        cx.wait(pe, "act")
                        mm = pe.matmul(prow(2048 + tt * 512, 512), LNCb[:, 0:1],
                                       SBF.ap()[:, 1032:1032 + 512],
                                       start=(oc == 0), stop=False,
                                       skip_group_check=True)
                        mm = pe.matmul(prow(3072 + tt * 512, 512), LNCb[:, 1:2],
                                       SBF.ap()[:, 0:512],
                                       start=(oc == 0), stop=False,
                                       skip_group_check=True)
                        cx.inc(mm, "hst", 1)
                cx.fence(ALL_E, ["hxo", "hst", "dve", "act"])
                _fe("p3%s" % "ab"[tt])
            _lns(tag)

        ln_apply(B[0])
        w2x_prep()
        ffn_eval("f1", B[0], first_eval=True, dst_dram=xdram, next_ln=True)

        if PHASE_LIMIT == 2:
            finalize_trunk()
            return nc

        ln_apply(B[0])
        XN3 = lambda kc: B[0][:, kc * T:(kc + 1) * T]
        Qh = lambda h: B[1][:, h * T:(h + 1) * T]
        qkvv = E["wqkv_qk"]
        kinv = ag_kin[0:2048, :].rearrange("(oc op) t -> oc op t", op=128)
        pfence()
        _ens("qk")
        b0 = cx.base("hw0", "hw1", "hpe", "hst")
        actc = []
        QS = lambda s: B[2][:, s * 1024:(s + 1) * 1024]
        for j in range(32):
            buf = j % 2
            if j >= 2:
                cx.wait(ge, "hpe", b0["hpe"] + j - 1)
            cx.inc(ge.dma_start(WS[:, buf * 8192: buf * 8192 + 2048],
                                qkvv[j]), f"hw{buf}", 16)
            cx.wait(pe, f"hw{buf}", cx.val(f"hw{buf}"))
            if j >= 2:
                cx.wait(pe, "act", actc[j - 2])
            pb = (j % 2) * 2
            for kc in range(KC):
                lhsT = WS[:, buf * 8192 + kc * 128: buf * 8192 + kc * 128 + 128]
                for i in range(2):
                    mm = pe.matmul(psb(pb + i), lhsT,
                                   XN3(kc)[:, i * 512: i * 512 + 512],
                                   start=(kc == 0), stop=False,
                                   skip_group_check=True)
            cx.inc(mm, "hpe", 1)
            cx.wait(se, "hpe", b0["hpe"] + j + 1)
            if j >= 2:
                cx.wait(se, "hst", b0["hst"] + j - 1)
            if j < 16:
                i1 = se.activation(Qh(j), psb(pb, 1024), AF.Identity,
                                   bias=CW["bqkv_c"][:, j:j + 1])
                cx.inc(i1, "act", 1)
            else:
                sl = buf * 1032
                cx.wait(se, f"kx{buf}", cx.val(f"kx{buf}"))
                i1 = se.activation(SBF.ap()[:, sl:sl + 1024], psb(pb, 1024),
                                   AF.Identity, bias=CW["bqkv_c"][:, j:j + 1])
                cx.inc(i1, "act", 1)
                cx.wait(sy, "act")
                cx.inc(sy.dma_start(kinv[j - 16], SBF.ap()[:, sl:sl + 1024]),
                       f"kx{buf}", 16)
            i2 = se.activation(QS(buf), psb(pb, 1024), AF.Square)
            cx.inc(i2, "act", 1)
            actc.append(cx.val("act"))
            cx.wait(pe, "act", actc[j])
            for i in range(2):
                mm = pe.matmul(pcmp((4 if j < 16 else 6) * 512 + i * 512, 512),
                               csm(j % 16), QS(buf)[:, i * 512: i * 512 + 512],
                               start=(j % 16 == 0), stop=False,
                               skip_group_check=True)
            cx.inc(mm, "hst", 1)
        cx.wait(se, "hst", b0["hst"] + 32)
        cx.inc(se.copy(CPa[:], pcmp(2048, 1024)), "act", 1)
        cx.inc(se.copy(CPb[:], pcmp(3072, 1024)), "act", 1)
        cx.wait(ve, "act")
        cx.wait(ve, "kx0")
        cx.wait(ve, "kx1")
        cx.inc(ve.tensor_copy(SBF.ap()[0:16, 0:1024], CPb[:]), "dve", 1)
        cx.wait(sy, "dve")
        cx.inc(sy.dma_start(ag_kin[4096:4112, :], SBF.ap()[0:16, 0:1024]),
               "kx0", 16)
        cx.fence(ALL_E, ["kx0", "kx1", "act", "dve"])
        _lns("qk")

        BQKVRb = RBF.ap()[0:1, 0:2048]
        cx.inc(ge.dma_start(BQKVRb, E["bqkv_r"][:]), "inig", 16)
        cx.fence([pe], ["inig"])
        _ens("vt")
        vTv = ag_kin[2048:4096, :].rearrange("(t two) q -> t (two q)", two=2)
        b0 = cx.base("hw0", "hw1", "hpe", "hxo")
        actc = []
        idx = 0
        for vc in range(8):
            buf = vc % 2
            if vc >= 2:
                cx.wait(ge, "hpe", b0["hpe"] + (vc - 1) * 8)
            cx.inc(ge.dma_start(WS[:, buf * 8192: buf * 8192 + 4096],
                                E["wqkv_v"][vc]), f"hw{buf}", 16)
            cx.wait(pe, f"hw{buf}", cx.val(f"hw{buf}"))
            for tb in range(8):
                pb = idx % 4
                if idx >= 4:
                    cx.wait(pe, "act", actc[idx - 4])
                for kc in range(KC):
                    lhsT = XN3(kc)[:, tb * 128:(tb + 1) * 128]
                    mm = pe.matmul(psb(pb, 256), lhsT,
                                   WS[:, buf * 8192 + kc * 256: buf * 8192 + kc * 256 + 256],
                                   start=(kc == 0), stop=False,
                                   skip_group_check=True)
                mm = pe.matmul(psb(pb, 256), ONESb[0:1, 0:128],
                               BQKVRb[0:1, vc * 256: vc * 256 + 256],
                               start=False, stop=False, skip_group_check=True)
                cx.inc(mm, "hpe", 1)
                cx.wait(se, "hpe", b0["hpe"] + idx + 1)
                sl = (idx % 2) * 1032
                cx.wait(se, f"kx{idx % 2}", cx.val(f"kx{idx % 2}"))
                i1 = se.activation(SBF.ap()[:, sl:sl + 256], psb(pb, 256), AF.Identity)
                cx.inc(i1, "act", 1)
                actc.append(cx.val("act"))
                cx.wait(sy, "act")
                cx.inc(sy.dma_start(vTv[tb * 128:(tb + 1) * 128,
                                        vc * 256: vc * 256 + 256],
                                    SBF.ap()[:, sl:sl + 256]), f"kx{idx % 2}", 16)
                idx += 1
        cx.fence(ALL_E, ["kx0", "kx1", "act"])

        _lns("vt")
        _ens("exch")
        cx.inc(ge.collective_compute(
            "AllGather", ALU.bypass, ins=[ag_kin.opt()], outs=[ag_kout.opt()],
            replica_groups=[[0, 1], [2, 3], [4, 5], [6, 7]]), "cc", 1)
        cx.fence(ALL_E, ["cc"])
        cx.inc(sy.dma_start(SBF.ap()[0:16, 0:1024], ag_kout[4096:4112, :]), "ag2", 16)
        cx.inc(sy.dma_start(SBF.ap()[0:16, 1032:1032 + 1024],
                            ag_kout[8208:8224, :]), "ag2", 16)
        cx.wait(ve, "ag2")
        cx.inc(ve.tensor_tensor(SBF.ap()[0:16, 0:1024], SBF.ap()[0:16, 0:1024],
                                SBF.ap()[0:16, 1032:1032 + 1024], ALU.max), "dve", 1)
        cx.wait(ve, "dve")
        cx.inc(ve.reduce_max(CPb[:, 0:1], SBF.ap()[0:16, 0:1024], axis=AX.X),
               "dve", 1)
        cx.inc(ve.reduce_max(CPa[:, 0:1], CPa[:], axis=AX.X), "dve", 1)
        cx.wait(se, "dve")
        cx.inc(se.activation(CPa[:, 0:1], CPa[:, 0:1], AF.Ln, bias=np.float32(1e-30)),
               "act", 1)
        cx.inc(se.activation(CPb[:, 0:1], CPb[:, 0:1], AF.Ln, bias=np.float32(1e-30)),
               "act", 1)
        cx.wait(ve, "act")
        cx.inc(ve.tensor_add(CPa[:, 0:1], CPa[:, 0:1], CPb[:, 0:1]), "dve", 1)
        cx.wait(se, "dve")
        cx.inc(se.activation(CPa[:, 0:1], CPa[:, 0:1], AF.Exp, scale=np.float32(0.5)),
               "act", 1)
        cx.wait(ve, "act")
        cx.inc(ve.tensor_scalar(CPa[:, 0:1], CPa[:, 0:1],
                                float(-np.sqrt(128.0)), None, ALU.mult), "dve", 1)
        mrow = nc.dram_tensor("mrow", [16, 1], F32, kind="Internal").ap()
        cx.wait(sy, "dve")
        cx.inc(sy.dma_start(mrow, CPa[:, 0:1]), "mx", 16)
        cx.wait(sy, "mx")
        cx.inc(sy.dma_start(R1[0:1, 0:16], mrow.rearrange("a b -> b a")), "mx", 16)
        cx.wait(ve, "mx")
        cx.inc(ve.tensor_copy(RBF.ap()[0:1, 2048:2064], R1[0:1, 0:16]), "dve", 1)
        cx.wait(pe, "dve")
        mm = pe.matmul(pcol(2048, 16), ONESb[0:1, 0:128], RBF.ap()[0:1, 2048:2064],
                       start=True, stop=False, skip_group_check=True)
        cx.inc(mm, "pe", 1)
        cx.wait(se, "pe")
        cx.inc(se.copy(MHall.ap(), pcol(2048, 16)), "act", 1)
        cx.wait(se, "act")
        _lns("exch")

        _ens("heads")
        AOUT = lambda h: B[3][:, h * T:(h + 1) * T]
        EST = lambda s: B[2][:, s * T:(s + 1) * T]
        vT0 = ag_kout[2048:4096, :].rearrange("(t two) q -> t (two q)", two=2)
        vT1 = ag_kout[6160:8208, :].rearrange("(t two) q -> t (two q)", two=2)
        bhh = cx.base("hev")
        pecnt = [bhh["hev"]]

        def head_loads(h):
            hb = (h % 2) * 8192
            if h >= 2:
                cx.wait(sy, "hev", pecnt[h - 1])
            cx.inc(sy.dma_start(WS[:, hb:hb + 1024],
                                ag_kout[h * 128:(h + 1) * 128, :]), f"hk{h % 2}", 16)
            cx.inc(sy.dma_start(WS[:, hb + 1024:hb + 2048],
                                ag_kout[4112 + h * 128: 4112 + h * 128 + 128, :]),
                   f"hk{h % 2}", 16)
            cx.inc(sy.dma_start(
                WS.ap()[:, hb + 2048:hb + 3072].rearrange("p (tb q) -> p tb q", tb=8),
                vT0.rearrange("(tb p) q -> tb p q", p=128)[:, :, h * 128:(h + 1) * 128]),
                f"hk{h % 2}", 16)
            cx.inc(sy.dma_start(
                WS.ap()[:, hb + 3072:hb + 4096].rearrange("p (tb q) -> p tb q", tb=8),
                vT1.rearrange("(tb p) q -> tb p q", p=128)[:, :, h * 128:(h + 1) * 128]),
                f"hk{h % 2}", 16)

        head_loads(0)
        prev_act = None
        rbf_dve = None
        evac_act = [None]
        tt_dve = [None]
        last_bcast_pe = cx.val("pe")

        def tail_b(hprev):
            nonlocal last_bcast_pe
            cx.wait(pe, "dve", rbf_dve)
            for i in range(2):
                mm = pe.matmul(pcol(2048 + i * 512, 512), ONESb[0:1, 0:128],
                               RBF.ap()[0:1, i * 512: i * 512 + 512],
                               start=True, stop=False, skip_group_check=True)
            cx.inc(mm, "pe", 1)
            last_bcast_pe = cx.val("pe")
            cx.wait(se, "pe")
            if hprev >= 1 and tt_dve[hprev - 1] is not None:
                cx.wait(se, "dve", tt_dve[hprev - 1])
            cx.inc(se.copy(SBF.ap()[:, 0:1024], pcol(2048, 1024)), "act", 1)
            evac_act.append(cx.val("act"))
            cx.wait(ve, "act", evac_act[-1])
            cx.wait(ve, "hev", pecnt[hprev + 1])
            cx.inc(ve.tensor_tensor(AOUT(hprev), pcol(3072, 1024),
                                    SBF.ap()[:, 0:1024], ALU.mult), "dve", 1)
            tt_dve.append(cx.val("dve"))

        for h in range(16):
            hb = (h % 2) * 8192
            bh = cx.base("hpe", "hev")
            actc2 = []
            cx.wait(pe, f"hk{h % 2}", cx.val(f"hk{h % 2}"))

            def ST(tb):
                pb = (tb % 2) * 2
                if tb >= 2:
                    cx.wait(pe, "act", actc2[tb - 2])
                elif prev_act is not None:
                    cx.wait(pe, "act", prev_act[14 + tb])
                kt = WS[:, hb + tb * 128: hb + (tb + 1) * 128] if tb < 8 else \
                    WS[:, hb + 1024 + (tb - 8) * 128: hb + 1024 + (tb - 7) * 128]
                for i in range(2):
                    mm = pe.matmul(psb(pb + i), kt,
                                   Qh(h)[:, i * 512: i * 512 + 512],
                                   start=True, stop=False, skip_group_check=True)
                cx.inc(mm, "hpe", 1)
                cx.wait(se, "hpe", bh["hpe"] + tb + 1)
                if tb >= 2:
                    cx.wait(se, "hev", bh["hev"] + tb - 1)
                i1 = se.activation(EST(tb % 2), psb(pb, 1024), AF.Exp,
                                   bias=MHall.ap()[:, h:h + 1],
                                   scale=np.float32(RSQ))
                cx.inc(i1, "act", 1)
                actc2.append(cx.val("act"))

            def SPV(tb):
                cx.wait(pe, "act", actc2[tb])
                if tb == 0 and evac_act[h] is not None:
                    cx.wait(pe, "act", evac_act[h])
                if tb == 0 and tt_dve[h] is not None:
                    cx.wait(pe, "dve", tt_dve[h])
                for i in range(2):
                    mm = pe.matmul(prow(2048 + i * 512, 512), LNCb[:, 2:3],
                                   EST(tb % 2)[:, i * 512: i * 512 + 512],
                                   start=(tb == 0), stop=False,
                                   skip_group_check=True)
                vt = WS[:, hb + 2048 + tb * 128: hb + 2048 + (tb + 1) * 128] \
                    if tb < 8 else \
                    WS[:, hb + 3072 + (tb - 8) * 128: hb + 3072 + (tb - 7) * 128]
                for i in range(2):
                    mm = pe.matmul(pcol(3072 + i * 512, 512), vt,
                                   EST(tb % 2)[:, i * 512: i * 512 + 512],
                                   start=(tb == 0), stop=False,
                                   skip_group_check=True)
                cx.inc(mm, "hev", 1)

            ST(0)
            if h + 1 < 16:
                head_loads(h + 1)
            ST(1)
            if h >= 1:
                tail_b(h - 1)
            ST(2)
            SPV(0)
            for tb in range(3, 16):
                ST(tb)
                SPV(tb - 2)
            SPV(14)
            SPV(15)
            pecnt.append(cx.val("hev"))
            prev_act = actc2
            cx.wait(se, "hev", bh["hev"] + 16)
            if rbf_dve is not None:
                cx.wait(se, "dve", rbf_dve)
            cx.inc(se.activation(R1[0:1, 0:1024], prow(2048, 1024), AF.Ln,
                                 bias=np.float32(1e-38)), "act", 1)
            cx.wait(se, "act")
            cx.inc(se.activation(R1[0:1, 0:1024], R1[0:1, 0:1024], AF.Exp,
                                 scale=np.float32(-1.0)), "act", 1)
            cx.wait(ve, "act")
            cx.wait(ve, "pe", last_bcast_pe)
            cx.inc(ve.tensor_copy(RBF.ap()[0:1, 0:1024], R1[0:1, 0:1024]), "dve", 1)
            rbf_dve = cx.val("dve")
        tail_b(15)
        cx.fence(ALL_E, ["dve", "act"])
        _lns("heads")

        with nc.named_scope("proj"):
            residual_proj("pj", E["wproj"], AOUT,
                          lambda oc: xdram.rearrange("(oc op) t -> oc op t", op=128)[oc],
                          xdram, bias_cols=CW["bproj"], next_ln=True)

        if PHASE_LIMIT == 3:
            finalize_trunk()
            return nc

        ln_apply(B[0])
        ffn_eval("f2", B[0], first_eval=False, dst_dram=y_out, next_ln=False)

    return nc


# version 08 (final)
# speedup vs baseline: 1.4728x; 1.0700x over previous
import numpy as np
import ml_dtypes
from contextlib import ExitStack

import concourse.bass as bass
import concourse.mybir as mybir
from concourse.bass_utils import run_bass_kernel_spmd

F32 = mybir.dt.float32
BF16 = mybir.dt.bfloat16
ALU = mybir.AluOpType
AF = mybir.ActivationFunctionType
AX = mybir.AxisListType

T = 1024
TH = T + 3
C = 2048
KC = 16
NH = 8192
EPS = 1e-5
BW = 16 * 1028
RSQ = float(1.0 / np.sqrt(128.0))
SCL = float(1.0 / np.sqrt(NH))

PHASE_LIMIT = 4
FINE = False
DBG = []
FENCES = []


class Ctx:
    def __init__(self, nc, stack):
        self.nc = nc
        self.stack = stack
        self.sems = {}
        self.cnt = {}

    def s(self, name):
        if name not in self.sems:
            self.sems[name] = self.stack.enter_context(self.nc.semaphore(name))
            self.cnt[name] = 0
        return self.sems[name]

    def inc(self, inst, name, n=1):
        inst.then_inc(self.s(name), n)
        self.cnt[name] += n
        return self.cnt[name]

    def wait(self, eng, name, val=None):
        self.s(name)
        v = self.cnt[name] if val is None else val
        if v > 0:
            eng.wait_ge(self.sems[name], v)

    def val(self, name):
        self.s(name)
        return self.cnt[name]

    def base(self, *names):
        return {n: self.val(n) for n in names}

    def fence(self, engines, names):
        for name in names:
            self.s(name)
            if self.cnt[name] > 0:
                for e in engines:
                    e.wait_ge(self.sems[name], self.cnt[name])


def _had(n):
    h = np.array([[1.0]], dtype=np.float32)
    while h.shape[0] < n:
        h = np.block([[h, h], [h, -h]])
    return h.astype(np.float32)


def _perm_jnew():
    p = np.zeros(NH, dtype=np.int64)
    for c in range(64):
        for d in range(128):
            p[d * 64 + c] = c * 128 + d
    return p


def host_inputs(inp):
    f32 = np.float32
    xg = np.asarray(inp["L0_attn_a_x"], dtype=f32)

    def tr(a):
        return np.ascontiguousarray(np.asarray(a, f32).T)

    def slabbed(WT_ko, ocn):
        K, O = WT_ko.shape
        kcn = K // 128
        a = WT_ko.reshape(kcn, 128, ocn, O // ocn)
        return np.ascontiguousarray(a.transpose(2, 1, 0, 3).reshape(ocn, 128, K * (O // ocn) // (O // ocn) * (O // ocn) // 128 * 128 // kcn * kcn)[:, :, :]) if False else             np.ascontiguousarray(a.transpose(2, 1, 0, 3).reshape(ocn, 128, -1))

    bqkv = np.asarray(inp["L0_attn_b_b_qkv"], f32)
    convw_raw = np.asarray(inp["L0_attn_a_conv_w"], f32)
    convw = np.zeros((128, 64), f32)
    for kc in range(16):
        for k in range(4):
            convw[:, kc * 4 + k] = convw_raw[kc * 128:(kc + 1) * 128, 0, k]

    def cols(v):
        return np.ascontiguousarray(np.asarray(v, f32).reshape(16, 128).T)

    def col1(v):
        return np.ascontiguousarray(np.asarray(v, f32).reshape(128, 1))

    w2T = tr(inp["L0_ffn_W2"])
    SEL = np.zeros((16, 2048), f32)
    for h in range(16):
        SEL[h, h * 128:(h + 1) * 128] = 1.0
    CS = np.zeros((128, 512), f32)
    for h in range(16):
        CS[:, h * 16 + h] = 1.0 / 128.0
        CS[:, 256 + h * 16 + h] = -1.0 / 128.0
    LNC = np.zeros((128, 3), f32)
    LNC[:, 0] = -1.0 / 2048.0
    LNC[:, 1] = 1.0 / 2048.0
    LNC[:, 2] = 1.0

    wqkv_f = np.asarray(inp["L0_attn_b_W_qkv"], f32)
    shared = {
        "wqT": slabbed(tr(inp["L0_attn_a_Wq"]), 16),
        "wkT": slabbed(tr(inp["L0_attn_a_Wk"]), 16),
        "wvT": slabbed(tr(inp["L0_attn_a_Wv"]), 16),
        "woT": slabbed(tr(inp["L0_attn_a_Wo"]), 16),
        "igT": slabbed(tr(inp["L0_attn_a_ig_w"]), 16),
        "ogT": slabbed(tr(inp["L0_attn_a_og_w"]), 16),
        "gammaT": tr(inp["L0_attn_a_gamma_w"]),
        "wqkv_qk": slabbed(wqkv_f[:, :4096], 32),
        "wqkv_v": np.ascontiguousarray(
            wqkv_f[:, 4096:].reshape(16, 128, 8, 256).transpose(2, 1, 0, 3)
            .reshape(8, 128, 4096)),
        "wproj": slabbed(np.ascontiguousarray(wqkv_f[:, 0:0]) if False else
                         np.asarray(inp["L0_attn_b_W_proj"], f32), 16),
        "w1T": slabbed(tr(inp["L0_ffn_W1"]), 64),
        "w3T": slabbed(tr(inp["L0_ffn_W3"]), 64),
        "w2resh": np.ascontiguousarray(w2T[_perm_jnew()]),
        "convw": convw,
        "convb": cols(inp["L0_attn_a_conv_b"]),
        "igb": cols(inp["L0_attn_a_ig_b"]),
        "ogb": cols(inp["L0_attn_a_og_b"]),
        "gammab": np.ascontiguousarray(
            np.asarray(inp["L0_attn_a_gamma_b"], f32).reshape(16, 1)),
        "vng": col1(inp["L0_attn_a_vn_g"]), "vnb": col1(inp["L0_attn_a_vn_b"]),
        "mng": col1(inp["L0_attn_a_mn_g"]), "mnb": col1(inp["L0_attn_a_mn_b"]),
        "gng": cols(inp["L0_attn_a_gn_g"]), "gnb": cols(inp["L0_attn_a_gn_b"]),
        "bqkv_c": np.ascontiguousarray(bqkv[:4096].reshape(32, 128).T),
        "bqkv_r": np.ascontiguousarray(bqkv[4096:].reshape(1, 2048)),
        "bproj": cols(inp["L0_attn_b_b_proj"]),
        "H128": _had(128),
        "I2H64": np.kron(np.eye(2, dtype=f32), _had(64)).astype(f32),
        "SEL": SEL, "CS": CS, "LNC": LNC,
        "ones_r": np.ones((1, 128), f32),
    }
    maps = []
    for c in range(8):
        b, half = c // 2, c % 2
        xs = xg[b, half * 1024:(half + 1) * 1024].T
        halo = np.zeros((2048, 3), f32) if half == 0 else \
            np.ascontiguousarray(xg[b, 1021:1024].T)
        m = dict(shared)
        m["x"] = np.ascontiguousarray(np.concatenate([halo, xs], axis=1))
        m["mask"] = np.full((128, 16), float(half), f32)
        maps.append(m)
    return maps


def kernel(**inp):
    nc = build_program()
    in_maps = host_inputs(inp)
    res = run_bass_kernel_spmd(nc, in_maps, core_ids=list(range(8)))
    out = np.zeros((4, 2048, 2048), dtype=np.float32)
    for c in range(8):
        b, half = c // 2, c % 2
        out[b, half * 1024:(half + 1) * 1024, :] = res.results[c]["y"].T
    return out


def build_program():
    nc = bass.Bass("TRN2", target_bir_lowering=False)
    E = {}
    for name, shape in [
            ("x", [C, TH]), ("wqT", [16, 128, C]), ("wkT", [16, 128, C]),
            ("wvT", [16, 128, C]), ("woT", [16, 128, C]), ("igT", [16, 128, C]),
            ("ogT", [16, 128, C]), ("gammaT", [C, 16]),
            ("wqkv_qk", [32, 128, C]), ("wqkv_v", [8, 128, 4096]),
            ("wproj", [16, 128, C]), ("w1T", [64, 128, C]), ("w3T", [64, 128, C]),
            ("w2resh", [NH, C]),
            ("convw", [128, 64]), ("convb", [128, 16]), ("igb", [128, 16]),
            ("ogb", [128, 16]), ("gammab", [16, 1]), ("vng", [128, 1]),
            ("vnb", [128, 1]), ("mng", [128, 1]), ("mnb", [128, 1]),
            ("gng", [128, 16]), ("gnb", [128, 16]), ("bqkv_c", [128, 32]),
            ("bqkv_r", [1, 2048]), ("bproj", [128, 16]),
            ("H128", [128, 128]), ("I2H64", [128, 128]), ("SEL", [16, 2048]),
            ("CS", [128, 512]), ("LNC", [128, 3]), ("ones_r", [1, 128]),
            ("mask", [128, 16])]:
        E[name] = nc.dram_tensor(name, shape, F32, kind="ExternalInput").ap()
    y_out = nc.dram_tensor("y", [C, T], F32, kind="ExternalOutput").ap()
    xdram = nc.dram_tensor("xdram", [C, T], F32, kind="Internal").ap()
    w1bf = nc.dram_tensor("w1bf", [64, 128, C], BF16, kind="Internal").ap()
    w3bf = nc.dram_tensor("w3bf", [64, 128, C], BF16, kind="Internal").ap()
    w2xbf = nc.dram_tensor("w2xbf", [16, 128, 64 * 128], BF16, kind="Internal").ap()
    udram = nc.dram_tensor("udram", [NH, T], BF16, kind="Internal").ap()
    hdram = nc.dram_tensor("hdram", [NH, T], BF16, kind="Internal").ap()
    ag_sin = nc.dram_tensor("ag_sin", [1, 2048], F32, kind="Internal").ap()
    ag_sout = nc.dram_tensor("ag_sout", [2, 2048], F32, kind="Internal").ap()
    ag_kin = nc.dram_tensor("ag_kin", [4112, 1024], BF16, kind="Internal").ap()
    ag_kout = nc.dram_tensor("ag_kout", [8224, 1024], BF16, kind="Internal").ap()

    with ExitStack() as stack:
        cx = Ctx(nc, stack)
        sb = lambda n, sh, dt: stack.enter_context(nc.sbuf_tensor(n, sh, dt))
        BP = sb("BP", [128, 4 * BW], BF16)
        B = [BP.ap()[:, i * BW:(i + 1) * BW] for i in range(4)]
        SC1 = sb("SC1", [128, 1032], F32)
        SC2 = SC1
        SBF = sb("SBF", [128, 2 * 1032], BF16)
        WS = sb("WS", [128, 2 * 8192], BF16)
        CPa = sb("CPa", [16, 1024], F32)
        CPb = sb("CPb", [16, 1024], F32)
        CPc = sb("CPc", [16, 1024], F32)
        ACf = sb("ACf", [16, 1024], F32)
        CPd = SBF.ap()[0:16, 0:2048].bitcast(F32)
        ZC = SBF.ap()[0:16, 0:1024]
        PCb = sb("PCb", [16, 1024], BF16)
        R1 = sb("R1", [1, 1032], F32)
        R2 = sb("R2", [1, 1032], F32)
        R3 = SC1.ap()[0:1, 0:1032]
        RBF = sb("RBF", [1, 2 * 1040], BF16)
        H128b = sb("H128b", [128, 128], BF16)
        I2H64b = sb("I2H64b", [128, 128], BF16)
        SELb = sb("SELb", [16, 2048], BF16)
        CSb = sb("CSb", [128, 512], BF16)
        LNCb = sb("LNCb", [128, 3], BF16)
        ONESb = sb("ONESb", [1, 128], BF16)
        BQKVRb = None
        CW = {}
        for n, sh in [("convw", [128, 64]), ("convb", [128, 16]),
                      ("igb", [128, 16]), ("ogb", [128, 16]),
                      ("gammab", [16, 1]), ("vng", [128, 1]), ("vnb", [128, 1]),
                      ("mng", [128, 1]), ("mnb", [128, 1]), ("gng", [128, 16]),
                      ("gnb", [128, 16]), ("bqkv_c", [128, 32]),
                      ("bproj", [128, 16]), ("mask", [128, 16])]:
            CW[n] = sb("c_" + n, sh, F32)
        SCOL = sb("SCOL", [128, 16], F32)
        SIN = sb("SIN", [128, 16], F32)
        MHall = sb("MHall", [128, 16], F32)
        PS = stack.enter_context(nc.psum_tensor("PS", [128, 4096], F32))

        pe, ve, se, ge, sy = nc.tensor, nc.vector, nc.scalar, nc.gpsimd, nc.sync
        _sids = {}

        def _ens(name):
            _sids[name] = nc.enter_named_scope(name, False)[0]

        def _lns(name):
            nc.leave_named_scope(name, _sids.pop(name), False)
        ALL_E = [pe, ve, se, ge, sy]

        def pcol(c0, n):
            return PS.ap()[:, c0:c0 + n]

        def prow(c0, n):
            return PS.ap()[0:1, c0:c0 + n]

        def pcmp(c0, n):
            return PS.ap()[0:16, c0:c0 + n]

        def psb(b0, n=512):
            return PS.ap()[:, b0 * 512: b0 * 512 + n]

        def csm(h):  return CSb[:, h * 16: h * 16 + 16]
        def csmn(h): return CSb[:, 256 + h * 16: 256 + h * 16 + 16]
        def selh(h): return SELb[:, h * 128:(h + 1) * 128]

        def pfence():
            cx.fence(ALL_E, ["pe", "dve", "act"])

        def tap(name, ap, cols, dt=BF16):
            if name in FENCES:
                cx.fence(ALL_E, ["pe", "dve", "act", "hpe", "hst", "hxo"])
            if name not in DBG:
                return
            dd = nc.dram_tensor("dbg_" + name, [ap.shape[0], cols], dt,
                                kind="ExternalOutput").ap()
            cx.fence(ALL_E, ["pe", "dve", "act", "hpe", "hst", "hxo"])
            cx.inc(sy.dma_start(dd, ap), "dbg", 16)
            cx.wait(sy, "dbg")

        for n in CW:
            cx.inc(sy.dma_start(CW[n][:], E[n][:]), "ini", 16)
        for dst, srcn in [(H128b, "H128"), (I2H64b, "I2H64"), (SELb, "SEL"),
                          (CSb, "CS"), (LNCb, "LNC"), (ONESb, "ones_r")]:
            cx.inc(ge.dma_start(dst[:], E[srcn][:]), "inig", 16)
        cx.wait(ve, "inig")
        xv = E["x"].rearrange("(kc kp) t -> kc kp t", kp=128)
        for kc in range(KC):
            cx.inc(ge.dma_start(BP.ap()[:, kc * 1028: kc * 1028 + TH], xv[kc]),
                   "xcast", 16)

        XNH = lambda kc: BP.ap()[:, kc * 1028: kc * 1028 + TH]
        XN = lambda kc: BP.ap()[:, kc * 1028 + 3: kc * 1028 + 3 + T]

        def ln_rows(n, m_col, sq_col):
            cx.inc(se.copy(R1[0:1, 0:n], prow(m_col, n)), "act", 1)
            cx.inc(se.copy(R2[0:1, 0:n], prow(sq_col, n)), "act", 1)
            cx.wait(ve, "act")
            cx.inc(ve.tensor_mul(R3[0:1, 0:n], R1[0:1, 0:n], R1[0:1, 0:n]), "dve", 1)
            cx.wait(ve, "dve")
            cx.inc(ve.tensor_sub(R2[0:1, 0:n], R2[0:1, 0:n], R3[0:1, 0:n]), "dve", 1)
            cx.wait(se, "dve")
            cx.inc(se.activation(R2[0:1, 0:n], R2[0:1, 0:n], AF.Ln, bias=np.float32(EPS)), "act", 1)
            cx.wait(se, "act")
            cx.inc(se.activation(R2[0:1, 0:n], R2[0:1, 0:n], AF.Exp, scale=np.float32(-0.5)), "act", 1)
            cx.wait(ve, "act")
            cx.inc(ve.tensor_mul(R1[0:1, 0:n], R1[0:1, 0:n], R2[0:1, 0:n]), "dve", 1)
            cx.wait(ve, "dve")
            cx.inc(ve.tensor_copy(RBF.ap()[0:1, 0:n], R2[0:1, 0:n]), "dve", 1)
            cx.wait(ve, "dve")
            cx.inc(ve.tensor_copy(RBF.ap()[0:1, 1040:1040 + n], R1[0:1, 0:n]), "dve", 1)
            cx.wait(pe, "dve")

        def bcast_rows(n, a_col, b_col):
            for t0 in range(0, n, 512):
                tl = min(512, n - t0)
                mm = pe.matmul(pcol(a_col + t0, tl), ONESb[0:1, 0:128],
                               RBF.ap()[0:1, t0:t0 + tl],
                               start=True, stop=False, skip_group_check=True)
                mm = pe.matmul(pcol(b_col + t0, tl), ONESb[0:1, 0:128],
                               RBF.ap()[0:1, 1040 + t0:1040 + t0 + tl],
                               start=True, stop=False, skip_group_check=True)
            cx.inc(mm, "pe", 1)

        _ens("ln1")
        cx.fence([se, pe, ve], ["xcast", "inig", "ini", "dve"])
        chunks3 = [(0, 512), (512, 512), (1024, 3)]
        bs = cx.base("l1st")
        for kc in range(KC):
            sl = (kc % 2) * 1032
            if kc >= 2:
                cx.wait(se, "l1st", bs["l1st"] + kc - 1)
            cx.inc(se.activation(SBF.ap()[:, sl:sl + TH], XNH(kc), AF.Square),
                   "act", 1)
            cx.wait(pe, "act")
            for (t0, tl) in chunks3:
                mm = pe.matmul(prow(0 + t0, tl), LNCb[:, 0:1],
                               XNH(kc)[:, t0:t0 + tl],
                               start=(kc == 0), stop=False,
                               skip_group_check=True)
                mm = pe.matmul(prow(1536 + t0, tl), LNCb[:, 1:2],
                               SBF.ap()[:, sl + t0: sl + t0 + tl],
                               start=(kc == 0), stop=False,
                               skip_group_check=True)
            cx.inc(mm, "l1st", 1)
        cx.fence([se], ["l1st"])
        ln_rows(TH, m_col=0, sq_col=1536)
        bcast_rows(TH, a_col=0, b_col=2048)
        cx.wait(ve, "pe")
        for kc in range(KC):
            xb = XNH(kc)
            cx.inc(ve.tensor_tensor(xb, xb, pcol(0, TH), ALU.mult), "dve", 1)
            cx.wait(ve, "dve")
            cx.inc(ve.tensor_tensor(xb, xb, pcol(2048, TH), ALU.add), "dve", 1)
            cx.wait(ve, "dve")

        tap("xn", BP.ap()[:, 0:16 * 1028], 16 * 1028)
        _lns("ln1")
        _ens("conv")
        for kc in range(KC):
            acc = SC1.ap()[:, 0:T]
            cx.wait(ve, "act")
            cx.inc(ve.tensor_scalar(acc, BP.ap()[:, kc * 1028: kc * 1028 + T],
                                    CW["convw"][:, kc * 4: kc * 4 + 1], None,
                                    ALU.mult), "dve", 1)
            for k in range(1, 4):
                cx.wait(ve, "dve")
                cx.inc(ve.scalar_tensor_tensor(
                    acc, BP.ap()[:, kc * 1028 + k: kc * 1028 + k + T],
                    CW["convw"][:, kc * 4 + k: kc * 4 + k + 1], acc,
                    ALU.mult, ALU.add), "dve", 1)
            cx.wait(se, "dve")
            cx.inc(se.activation(B[1][:, kc * T:(kc + 1) * T], acc, AF.Silu,
                                 bias=CW["convb"][:, kc:kc + 1]), "act", 1)
        XC = lambda kc: B[1][:, kc * T:(kc + 1) * T]

        _lns("conv")
        _ens("gamma")
        cx.wait(ge, "act")
        cx.inc(ge.dma_start(WS[:, 0:256],
                            E["gammaT"].rearrange("(kc kp) h -> kp kc h", kp=128)),
               "gw", 16)
        cx.fence([pe], ["gw", "act", "dve"])
        for kc in range(KC):
            for i in range(2):
                mm = pe.matmul(pcmp(i * 512, 512), WS[:, kc * 16: kc * 16 + 16],
                               XC(kc)[:, i * 512: i * 512 + 512],
                               start=(kc == 0), stop=False, skip_group_check=True)
        cx.inc(mm, "pe", 1)
        cx.wait(se, "pe")
        cx.inc(se.activation(ACf[:], pcmp(0, 1024), AF.Sigmoid,
                             bias=CW["gammab"][:]), "act", 1)
        cx.wait(ve, "act")
        cx.inc(ve.memset(ZC[:], 0.0), "dve", 1)
        cx.wait(ve, "dve")
        cx.inc(ve.tensor_tensor_scan(PCb[:], ACf[:], ZC[:], 1,
                                     ALU.mult, ALU.add), "dve", 1)

        tap("xc", B[1][:, 0:16 * T], 16 * T)
        tap("a", ACf[:], 1024, F32)
        tap("p", PCb[:], 1024, BF16)
        _lns("gamma")
        def head_proj(tag, wsrc, dst, want_mean, want_sq, act_fn=AF.Copy,
                      bias=None, rhs=None):
            if rhs is None:
                rhs = XN
            wv = wsrc
            b0 = cx.base("hw0", "hw1", "hpe", "hst")
            info = []
            actc = []
            for h in range(17):
                if h < 16:
                    buf = h % 2
                    if h >= 2:
                        cx.wait(ge, "hpe", b0["hpe"] + h - 1)
                    cx.inc(ge.dma_start(WS[:, buf * 8192: buf * 8192 + 2048],
                                        wv[h]), f"hw{buf}", 16)
                    cx.wait(pe, f"hw{buf}", cx.val(f"hw{buf}"))
                    if h >= 2:
                        cx.wait(pe, "act", actc[h - 2])
                    pb = (h % 2) * 2
                    for kc in range(KC):
                        lhsT = WS[:, buf * 8192 + kc * 128: buf * 8192 + kc * 128 + 128]
                        for i in range(2):
                            mm = pe.matmul(psb(pb + i), lhsT,
                                           rhs(kc)[:, i * 512: i * 512 + 512],
                                           start=(kc == 0), stop=False,
                                           skip_group_check=True)
                    cx.inc(mm, "hpe", 1)
                    cx.wait(se, "hpe", b0["hpe"] + h + 1)
                    if h >= 2:
                        cx.wait(se, "hst", b0["hst"] + h - 1)
                    sl = buf * 1032
                    kw = {}
                    if bias is not None:
                        kw["bias"] = bias[:, h:h + 1]
                    i1 = se.activation(dst(h), psb(pb, 1024), act_fn, **kw)
                    cx.inc(i1, "act", 1)
                    if want_sq:
                        i2 = se.activation(SBF.ap()[:, sl:sl + 1024], psb(pb, 1024),
                                           AF.Square)
                        cx.inc(i2, "act", 1)
                    actc.append(cx.val("act"))
                    info.append(sl)
                hs = h - 1
                if hs >= 0:
                    sl2 = info[hs]
                    cx.wait(pe, "act", actc[hs])
                    if want_mean:
                        for i in range(2):
                            mm = pe.matmul(pcmp(2048 + i * 512, 512), csmn(hs),
                                           dst(hs)[:, i * 512: i * 512 + 512],
                                           start=(hs == 0), stop=False,
                                           skip_group_check=True)
                    if want_sq:
                        csx = csm(hs)
                        for i in range(2):
                            mm = pe.matmul(pcmp(3072 + i * 512, 512), csx,
                                           SBF.ap()[:, sl2 + i * 512: sl2 + i * 512 + 512],
                                           start=(hs == 0), stop=False,
                                           skip_group_check=True)
                    cx.inc(mm, "hst", 1)
            cx.wait(se, "hst", b0["hst"] + 16)
            if want_mean:
                cx.inc(se.copy(CPa[:], pcmp(2048, 1024)), "act", 1)
            if want_sq:
                cx.inc(se.copy(CPb[:], pcmp(3072, 1024)), "act", 1)
            cx.fence([ve, pe], ["act"])

        def compact_rstd(dst, tmp, ln_style):
            if ln_style:
                cx.inc(ve.tensor_mul(tmp[:], CPa[:], CPa[:]), "dve", 1)
                cx.wait(ve, "dve")
                cx.inc(ve.tensor_sub(tmp[:], CPb[:], tmp[:]), "dve", 1)
                cx.wait(se, "dve")
                cx.inc(se.activation(tmp[:], tmp[:], AF.Ln, bias=np.float32(EPS)), "act", 1)
            else:
                cx.inc(ve.tensor_scalar_max(tmp[:], CPb[:], 7.8125e-27), "dve", 1)
                cx.wait(se, "dve")
                cx.inc(se.activation(tmp[:], tmp[:], AF.Ln, scale=np.float32(128.0),
                                     bias=np.float32(1e-38)), "act", 1)
            cx.wait(se, "act")
            cx.inc(se.activation(dst[:], tmp[:], AF.Exp, scale=np.float32(-0.5)), "act", 1)
            cx.fence([ve, pe], ["act"])

        def apply_affine(dst_fn, alpha_cp, beta_cp, g_col_fn, b_col_fn,
                         extra_mul_fn=None):
            ab_bf = SBF.ap()[0:16, 0:1024]
            bb_bf = SBF.ap()[0:16, 1032:1032 + 1024]
            cx.inc(ve.tensor_copy(ab_bf, alpha_cp[:]), "dve", 1)
            cx.wait(ve, "dve")
            if beta_cp is not None:
                cx.inc(ve.tensor_copy(bb_bf, beta_cp[:]), "dve", 1)
                cx.wait(ve, "dve")
            dvh = []
            for h in range(16):
                bo = (h % 2) * 2048
                if h >= 2:
                    cx.wait(pe, "dve", dvh[h - 2])
                else:
                    cx.fence([pe], ["dve"])
                for i in range(2):
                    mm = pe.matmul(pcol(bo + i * 512, 512), selh(h),
                                   ab_bf[:, i * 512: i * 512 + 512],
                                   start=True, stop=False, skip_group_check=True)
                    if beta_cp is not None:
                        mm = pe.matmul(pcol(bo + 1024 + i * 512, 512), selh(h),
                                       bb_bf[:, i * 512: i * 512 + 512],
                                       start=True, stop=False,
                                       skip_group_check=True)
                cx.inc(mm, "pe", 1)
                cx.wait(ve, "pe")
                d = dst_fn(h)
                cx.inc(ve.tensor_tensor(d, d, pcol(bo, 1024), ALU.mult), "dve", 1)
                cx.wait(ve, "dve")
                if beta_cp is not None:
                    cx.inc(ve.tensor_tensor(d, d, pcol(bo + 1024, 1024), ALU.add),
                           "dve", 1)
                    cx.wait(ve, "dve")
                if g_col_fn is not None:
                    cx.inc(ve.tensor_scalar(d, d, g_col_fn(h), b_col_fn(h),
                                            ALU.mult, ALU.add), "dve", 1)
                    cx.wait(ve, "dve")
                if extra_mul_fn is not None:
                    cx.inc(ve.tensor_tensor(d, d, extra_mul_fn(h), ALU.mult),
                           "dve", 1)
                    cx.wait(ve, "dve")
                dvh.append(cx.val("dve"))

        def gate_proj(tag, wsrc, rhs_fn, out_fn, bias_cols):
            wv = wsrc
            b0 = cx.base("hw0", "hw1", "hpe")
            actc = []
            dvec = []
            for h in range(16):
                buf = h % 2
                if h >= 2:
                    cx.wait(ge, "hpe", b0["hpe"] + h - 1)
                cx.inc(ge.dma_start(WS[:, buf * 8192: buf * 8192 + 2048],
                                    wv[h]), f"hw{buf}", 16)
                cx.wait(pe, f"hw{buf}", cx.val(f"hw{buf}"))
                if h >= 2:
                    cx.wait(pe, "act", actc[h - 2])
                pb = (h % 2) * 2
                for kc in range(KC):
                    lhsT = WS[:, buf * 8192 + kc * 128: buf * 8192 + kc * 128 + 128]
                    for i in range(2):
                        mm = pe.matmul(psb(pb + i), lhsT,
                                       rhs_fn(kc)[:, i * 512: i * 512 + 512],
                                       start=(kc == 0), stop=False,
                                       skip_group_check=True)
                cx.inc(mm, "hpe", 1)
                cx.wait(se, "hpe", b0["hpe"] + h + 1)
                if h >= 2:
                    cx.wait(se, "dve", dvec[h - 2])
                sl = buf * 1032
                i1 = se.activation(SBF.ap()[:, sl:sl + 1024], psb(pb, 1024),
                                   AF.Sigmoid, bias=bias_cols[:, h:h + 1])
                cx.inc(i1, "act", 1)
                actc.append(cx.val("act"))
                cx.wait(ve, "act")
                tt = out_fn(h, SBF.ap()[:, sl:sl + 1024])
                cx.inc(tt, "dve", 1)
                dvec.append(cx.val("dve"))
            cx.fence([pe, ve, se, ge], ["dve", "act"])

        V = lambda h: B[2][:, h * T:(h + 1) * T]
        pfence()
        with nc.named_scope("vproj"):
            head_proj("v", E["wvT"], V, want_mean=True, want_sq=True)
        compact_rstd(CPc, CPd, ln_style=True)
        cx.inc(ve.tensor_mul(CPa[:], CPa[:], CPc[:]), "dve", 1)
        cx.wait(ve, "dve")
        with nc.named_scope("vapply"):
            apply_affine(V, CPc, CPa, lambda h: CW["vng"][:, 0:1],
                         lambda h: CW["vnb"][:, 0:1])

        tap("vn", B[2][:, 0:16 * T], 16 * T)
        K_ = lambda h: B[3][:, h * T:(h + 1) * T]
        pfence()
        with nc.named_scope("kproj"):
            head_proj("k", E["wkT"], K_, want_mean=False, want_sq=True)
        compact_rstd(CPb, CPd, ln_style=False)
        apply_affine(K_, CPb, None, None, None)
        cx.inc(ve.tensor_mul(B[2][:, 0:16 * T], B[2][:, 0:16 * T],
                             B[3][:, 0:16 * T]), "dve", 1)
        cx.wait(ve, "dve")

        tap("kv", B[2][:, 0:16 * T], 16 * T)
        pfence()
        with nc.named_scope("ig"):
            gate_proj("ig", E["igT"], XC,
                      lambda h, sig: ve.tensor_tensor(K_(h), sig,
                                                      B[2][:, h * T:(h + 1) * T],
                                                      ALU.mult),
                      CW["igb"])
        BT = K_

        tap("b", B[3][:, 0:16 * T], 16 * T)
        MEM = lambda h: B[2][:, h * T:(h + 1) * T]
        pfence()

        def bcast_scan(src_cp, op_fn):
            b0 = cx.base("hpe")
            dvec = []
            for h in range(16):
                pb = (h % 2) * 2
                if h >= 2:
                    cx.wait(pe, "dve", dvec[h - 2])
                for i in range(2):
                    mm = pe.matmul(psb(pb + i), selh(h),
                                   src_cp[:, i * 512: i * 512 + 512],
                                   start=True, stop=False, skip_group_check=True)
                cx.inc(mm, "hpe", 1)
                cx.wait(ve, "hpe", b0["hpe"] + h + 1)
                op = op_fn(h, psb(pb, 1024))
                cx.inc(op, "dve", 1)
                dvec.append(cx.val("dve"))
            cx.fence([pe, ve], ["dve"])

        _ens("scan")
        b0s = cx.base("hpe", "hsl0", "hsl1")
        cx.fence([sy], ["dve", "act"])
        dvec = []
        for h in range(16):
            pb = (h % 2) * 2
            c0 = (h % 2) * 512
            if h >= 2:
                cx.wait(sy, "hpe", b0s["hpe"] + h - 1)
            cx.inc(sy.dma_start(SC1.ap()[0:16, c0:c0 + 128],
                                E["SEL"][:, h * 128:(h + 1) * 128]), f"hsl{h % 2}", 16)
            cx.wait(pe, f"hsl{h % 2}", cx.val(f"hsl{h % 2}"))
            if h >= 2:
                cx.wait(pe, "dve", dvec[h - 2])
            for i in range(2):
                mm = pe.matmul(psb(pb + i), SC1.ap()[0:16, c0:c0 + 128],
                               ACf[:, i * 512: i * 512 + 512],
                               start=True, stop=False, skip_group_check=True)
            cx.inc(mm, "hpe", 1)
            cx.wait(ve, "hpe", b0s["hpe"] + h + 1)
            op = ve.tensor_tensor_scan(MEM(h), psb(pb, 1024), BT(h), 0,
                                       ALU.mult, ALU.add)
            cx.inc(op, "dve", 1)
            dvec.append(cx.val("dve"))
        cx.fence([pe, ve, sy], ["dve"])
        _lns("scan")
        _ens("relay")
        ext = B[2][:, 0:16 * T].rearrange("p (h t) -> p h t", t=T)[:, :, T - 1:T]
        cx.inc(ve.tensor_copy(SCOL.ap().rearrange("p (h o) -> p h o", o=1), ext),
               "dve", 1)
        cx.wait(sy, "dve")
        cx.inc(sy.dma_start(ag_sin.rearrange("one (p h) -> (one p) h", p=128),
                            SCOL[:]), "ag1", 16)
        cx.wait(ge, "ag1")
        cx.inc(ge.collective_compute(
            "AllGather", ALU.bypass, ins=[ag_sin.opt()], outs=[ag_sout.opt()],
            replica_groups=[[0, 1], [2, 3], [4, 5], [6, 7]]), "cc", 1)
        cx.wait(sy, "cc")
        cx.inc(sy.dma_start(SIN[:],
                            ag_sout[0:1, :].rearrange("one (p h) -> (one p) h", p=128)),
               "ag2", 16)
        cx.wait(ve, "ag2")
        cx.inc(ve.tensor_mul(SIN[:], SIN[:], CW["mask"][:]), "dve", 1)
        cx.wait(ve, "dve")
        bcast_scan(PCb, lambda h, pb_: ve.scalar_tensor_tensor(
            MEM(h), pb_, SIN[:, h:h + 1], MEM(h), ALU.mult, ALU.add))
        _lns("relay")

        tap("mem", B[2][:, 0:16 * T], 16 * T)
        def sbuf_stats(src_fn, want_mean=True):
            b0 = cx.base("hst")
            for h in range(16):
                sl = (h % 2) * 1032
                if h >= 2:
                    cx.wait(se, "hst", b0["hst"] + h - 1)
                cx.inc(se.activation(SBF.ap()[:, sl:sl + 1024], src_fn(h),
                                     AF.Square), "act", 1)
                cx.wait(pe, "act")
                for i in range(2):
                    if want_mean:
                        mm = pe.matmul(pcmp(2048 + i * 512, 512), csmn(h),
                                       src_fn(h)[:, i * 512: i * 512 + 512],
                                       start=(h == 0), stop=False,
                                       skip_group_check=True)
                    mm = pe.matmul(pcmp(3072 + i * 512, 512),
                                   csm(h),
                                   SBF.ap()[:, sl + i * 512: sl + i * 512 + 512],
                                   start=(h == 0), stop=False,
                                   skip_group_check=True)
                cx.inc(mm, "hst", 1)
            cx.wait(se, "hst", b0["hst"] + 16)
            if want_mean:
                cx.inc(se.copy(CPa[:], pcmp(2048, 1024)), "act", 1)
            cx.inc(se.copy(CPb[:], pcmp(3072, 1024)), "act", 1)
            cx.fence([ve, pe], ["act"])

        pfence()
        _ens("mnstat")
        sbuf_stats(MEM, want_mean=True)
        compact_rstd(CPc, CPd, ln_style=True)
        cx.inc(ve.tensor_mul(CPa[:], CPa[:], CPc[:]), "dve", 1)
        cx.wait(ve, "dve")
        _lns("mnstat")

        Q_ = lambda h: B[3][:, h * T:(h + 1) * T]
        pfence()
        with nc.named_scope("qproj"):
            head_proj("q", E["wqT"], Q_, want_mean=False, want_sq=True)
        compact_rstd(CPb, CPd, ln_style=False)
        apply_affine(Q_, CPb, None, None, None)

        tap("qn", B[3][:, 0:16 * T], 16 * T)
        _ens("outapply")
        OUT = lambda h: B[0][:, h * T:(h + 1) * T]
        for h in range(16):
            cx.inc(ve.tensor_copy(OUT(h), MEM(h)), "dve", 1)
            cx.wait(ve, "dve")
        apply_affine(OUT, CPc, CPa, lambda h: CW["mng"][:, 0:1],
                     lambda h: CW["mnb"][:, 0:1], extra_mul_fn=Q_)

        _lns("outapply")
        _ens("gn")
        pfence()
        sbuf_stats(OUT, want_mean=True)
        compact_rstd(CPc, CPd, ln_style=True)
        cx.inc(ve.tensor_mul(CPa[:], CPa[:], CPc[:]), "dve", 1)
        cx.wait(ve, "dve")
        apply_affine(OUT, CPc, CPa, lambda h: CW["gng"][:, h:h + 1],
                     lambda h: CW["gnb"][:, h:h + 1])

        tap("out_gn", B[0][:, 0:16 * T], 16 * T)
        _lns("gn")
        pfence()
        with nc.named_scope("og"):
            gate_proj("og", E["ogT"], XC,
                      lambda h, sig: ve.tensor_tensor(OUT(h), OUT(h), sig, ALU.mult),
                      CW["ogb"])

        def residual_proj(tag, wsrc, rhs_fn, res_fn, dst_dram,
                          bias_cols=None, next_ln=True):
            wv = wsrc
            dstv = dst_dram.rearrange("(oc op) t -> oc op t", op=128)
            b0 = cx.base("hw0", "hw1", "hpe", "hst", "hxi", "hxo")
            dvec = []
            for oc in range(16):
                buf = oc % 2
                if oc >= 2:
                    cx.wait(ge, "hpe", b0["hpe"] + oc - 1)
                cx.inc(ge.dma_start(WS[:, buf * 8192: buf * 8192 + 2048],
                                    wv[oc]), f"hw{buf}", 16)
                cx.wait(pe, f"hw{buf}", cx.val(f"hw{buf}"))
                if oc >= 2:
                    cx.wait(pe, "dve", dvec[oc - 2])
                pb = (oc % 2) * 2
                for kc in range(KC):
                    lhsT = WS[:, buf * 8192 + kc * 128: buf * 8192 + kc * 128 + 128]
                    for i in range(2):
                        mm = pe.matmul(psb(pb + i), lhsT,
                                       rhs_fn(kc)[:, i * 512: i * 512 + 512],
                                       start=(kc == 0), stop=False,
                                       skip_group_check=True)
                cx.inc(mm, "hpe", 1)
                cx.wait(sy, "dve")
                cx.wait(sy, "act")
                if oc >= 1:
                    cx.wait(sy, "hxo", b0["hxo"] + 16 * oc)
                cx.inc(sy.dma_start(SC1.ap()[:, 0:T], res_fn(oc)), "hxi", 16)
                cx.wait(ve, "hxi")
                cx.wait(ve, "hpe", b0["hpe"] + oc + 1)
                cx.wait(ve, "hxo", b0["hxo"] + 16 * oc)
                cx.wait(ve, "act")
                tt_ins = ve.tensor_tensor(SC2.ap()[:, 0:T], psb(pb, 1024),
                                          SC1.ap()[:, 0:T], ALU.add)
                cx.inc(tt_ins, "dve", 1)
                dvec.append(cx.val("dve"))
                cx.wait(ve, "dve")
                if bias_cols is not None:
                    cx.inc(ve.tensor_scalar(SC2.ap()[:, 0:T], SC2.ap()[:, 0:T],
                                            bias_cols[:, oc:oc + 1], None,
                                            ALU.add), "dve", 1)
                    cx.wait(ve, "dve")
                cx.wait(sy, "dve")
                cx.inc(sy.dma_start(dstv[oc], SC2.ap()[:, 0:T]), "hxo", 16)
                if next_ln:
                    cx.wait(se, "dve")
                    cx.wait(se, "hst", b0["hst"] + oc)
                    cx.inc(se.activation(SBF.ap()[:, 0:1024], SC2.ap()[:, 0:T],
                                         AF.Square), "act", 1)
                    cx.inc(se.copy(SBF.ap()[:, 1032:1032 + 1024], SC2.ap()[:, 0:T]),
                           "act", 1)
                    cx.wait(pe, "act")
                    for i in range(2):
                        mm = pe.matmul(prow(2048 + i * 512, 512), LNCb[:, 0:1],
                                       SBF.ap()[:, 1032 + i * 512:1032 + i * 512 + 512],
                                       start=(oc == 0), stop=False,
                                       skip_group_check=True)
                        mm = pe.matmul(prow(3072 + i * 512, 512), LNCb[:, 1:2],
                                       SBF.ap()[:, i * 512: i * 512 + 512],
                                       start=(oc == 0), stop=False,
                                       skip_group_check=True)
                    cx.inc(mm, "hst", 1)
            cx.fence(ALL_E, ["hxo", "hst", "dve", "act"])

        xin_res = lambda oc: E["x"].rearrange("(oc op) t -> oc op t", op=128)[oc][:, 3:3 + T]
        with nc.named_scope("wo"):
            residual_proj("wo", E["woT"], OUT, xin_res, xdram, next_ln=True)

        def finalize_trunk():
            xdv = xdram.rearrange("(oc op) t -> oc op t", op=128)
            ydv = y_out.rearrange("(oc op) t -> oc op t", op=128)
            for oc in range(16):
                cx.inc(sy.dma_start(SC1.ap()[:, 0:T], xdv[oc]), "fcp", 16)
                cx.wait(sy, "fcp")
                cx.inc(sy.dma_start(ydv[oc], SC1.ap()[:, 0:T]), "fcp", 16)
                cx.wait(sy, "fcp")

        if PHASE_LIMIT == 1:
            finalize_trunk()
            return nc

        def ln_apply(dst_b):
            _ens("ln_apply")
            ln_rows(T, m_col=2048, sq_col=3072)
            bcast_rows(T, a_col=0, b_col=1024)
            xdv = xdram.rearrange("(kc kp) t -> kc kp t", kp=128)
            b0 = cx.base("xl0", "xl1", "pe")
            dvec = []
            for kc in range(KC):
                buf = kc % 2
                if kc >= 2:
                    cx.wait(ge, "dve", dvec[kc - 2])
                cx.inc(ge.dma_start(dst_b[:, kc * T:(kc + 1) * T], xdv[kc]),
                       f"xl{buf}", 16)
                cx.wait(ve, f"xl{buf}", cx.val(f"xl{buf}"))
                cx.wait(ve, "pe")
                cx.inc(ve.tensor_tensor(dst_b[:, kc * T:(kc + 1) * T],
                                        dst_b[:, kc * T:(kc + 1) * T],
                                        pcol(0, 1024), ALU.mult), "dve", 1)
                cx.wait(ve, "dve")
                cx.inc(ve.tensor_tensor(dst_b[:, kc * T:(kc + 1) * T],
                                        dst_b[:, kc * T:(kc + 1) * T],
                                        pcol(1024, 1024), ALU.add), "dve", 1)
                dvec.append(cx.val("dve"))
                cx.wait(ve, "dve")
            _lns("ln_apply")

        def w2x_prep():
            _ens("w2xprep")
            dstv = None
            cx.fence([pe], ["dve", "act"])
            b0 = cx.base("hw0", "hw1", "hpe")
            actc = []
            stc = []
            for G in range(64):
                buf = G % 2
                if G >= 2:
                    cx.wait(ge, "hpe", b0["hpe"] + G - 1)
                cx.inc(ge.dma_start(WS[:, buf * 8192: buf * 8192 + 2048],
                                    E["w2resh"][G * 128:(G + 1) * 128, :]),
                       f"hw{buf}", 16)
                cx.wait(pe, f"hw{buf}", cx.val(f"hw{buf}"))
                if G >= 2:
                    cx.wait(pe, "act", actc[G - 2])
                pb = (G % 2) * 4
                for i in range(4):
                    mm = pe.matmul(psb(pb + i), I2H64b[:],
                                   WS[:, buf * 8192 + i * 512: buf * 8192 + i * 512 + 512],
                                   start=True, stop=False, skip_group_check=True)
                cx.inc(mm, "hpe", 1)
                slot = G % 8
                sb_ = B[1][:, slot * 2048: slot * 2048 + 2048]
                cx.wait(se, "hpe", b0["hpe"] + G + 1)
                if G >= 8:
                    cx.wait(se, f"st{slot}", stc[G - 8])
                cx.inc(se.activation(sb_[:, 0:1024], psb(pb, 1024),
                                     AF.Identity, scale=np.float32(SCL)), "act", 1)
                i2 = se.activation(sb_[:, 1024:2048], psb(pb + 2, 1024),
                                   AF.Identity, scale=np.float32(SCL))
                cx.inc(i2, "act", 1)
                actc.append(cx.val("act"))
                cx.wait(sy, "act")
                dG = w2xbf[:, 2 * G:2 * G + 2, :].rearrange(
                    "oc dlow (kc ow) -> dlow kc oc ow", ow=128)
                cx.inc(sy.dma_start(dG, sb_.rearrange(
                    "p (oc ow) -> p oc ow", ow=128) if False else sb_),
                       f"st{slot}", 16)
                stc.append(cx.val(f"st{slot}"))
            cx.fence(ALL_E, [f"st{i}" for i in range(8)] + ["act"])
            _lns("w2xprep")

        HID = lambda jc: BP.ap()[:, BW + jc * 512: BW + jc * 512 + 512]

        def ffn_eval(tag, xn_b, first_eval, dst_dram, next_ln):
            _ens(tag)
            xn_fn = lambda kc: xn_b[:, kc * T:(kc + 1) * T]
            cx.fence([pe], ["dve", "act"])
            def _fs(nm):
                if FINE:
                    _ens(tag + nm)
            def _fe(nm):
                if FINE:
                    _lns(tag + nm)
            uv = udram.rearrange("(jc kp) t -> jc kp t", kp=128)
            hv = hdram.rearrange("(jc kp) t -> jc kp t", kp=128)
            for ip, (wname, wfp, wbf) in enumerate(
                    (("w1", E["w1T"], w1bf), ("w3", E["w3T"], w3bf))):
                wv = wfp
                wvb = wbf
                _fs("p%d" % (ip + 1))
                ldq = ge if first_eval else sy
                auxq = sy if first_eval else ge
                lds = "hw" if first_eval else "sw"
                fsp = "sf" if first_eval else "gf"
                b0 = cx.base("hpe", "htee0", "htee1")
                stc = {}
                ulc = {}
                actc = []
                for jc in range(64):
                    buf = jc % 2
                    slot = jc % 4
                    if jc >= 2:
                        cx.wait(ldq, "hpe", b0["hpe"] + jc - 1)
                        if first_eval:
                            cx.wait(ldq, f"htee{buf}", cx.val(f"htee{buf}"))
                    if first_eval:
                        cx.inc(ge.dma_start(WS[:, buf * 8192: buf * 8192 + 2048],
                                            wv[jc]), f"hw{buf}", 16)
                        cx.wait(sy, f"hw{buf}", cx.val(f"hw{buf}"))
                        cx.inc(sy.dma_start(wvb[jc],
                                            WS[:, buf * 8192: buf * 8192 + 2048]),
                               f"htee{buf}", 16)
                    else:
                        cx.inc(sy.dma_start(WS[:, buf * 8192: buf * 8192 + 2048],
                                            wvb[jc]), f"sw{buf}", 16)
                    cx.wait(pe, f"{lds}{buf}", cx.val(f"{lds}{buf}"))
                    pb = (jc % 2) * 2
                    if jc >= 2:
                        cx.wait(pe, "act" if ip == 0 else "dve", actc[jc - 2])
                    for kc in range(KC):
                        lhsT = WS[:, buf * 8192 + kc * 128: buf * 8192 + kc * 128 + 128]
                        for i in range(2):
                            mm = pe.matmul(psb(pb + i), lhsT,
                                           xn_fn(kc)[:, i * 512: i * 512 + 512],
                                           start=(kc == 0), stop=False,
                                           skip_group_check=True)
                    cx.inc(mm, "hpe", 1)
                    if ip == 0:
                        sb_ = B[3][:, slot * 1024: slot * 1024 + 1024]
                        cx.wait(se, "hpe", b0["hpe"] + jc + 1)
                        if jc >= 4:
                            cx.wait(se, f"{fsp}s{slot}", stc[jc - 4])
                        i1 = se.activation(sb_, psb(pb, 1024), AF.Silu)
                        cx.inc(i1, "act", 1)
                        actc.append(cx.val("act"))
                        cx.wait(auxq, "act")
                        cx.inc(auxq.dma_start(uv[jc], sb_), f"{fsp}s{slot}", 16)
                        stc[jc] = cx.val(f"{fsp}s{slot}")
                    else:
                        us_ = B[3][:, (4 + slot) * 1024: (4 + slot) * 1024 + 1024]
                        sb_ = B[3][:, slot * 1024: slot * 1024 + 1024]
                        if jc >= 4:
                            cx.wait(auxq, "dve", ulc[jc - 4])
                        cx.inc(auxq.dma_start(us_, uv[jc]), f"{fsp}u{slot}", 16)
                        cx.wait(ve, f"{fsp}u{slot}", cx.val(f"{fsp}u{slot}"))
                        cx.wait(ve, "hpe", b0["hpe"] + jc + 1)
                        if jc >= 4:
                            cx.wait(ve, f"{fsp}s{slot}", stc[jc - 4])
                        i1 = ve.tensor_tensor(sb_, psb(pb, 1024), us_, ALU.mult)
                        cx.inc(i1, "dve", 1)
                        actc.append(cx.val("dve"))
                        ulc[jc] = cx.val("dve")
                        cx.wait(auxq, "dve")
                        cx.inc(auxq.dma_start(hv[jc], sb_), f"{fsp}s{slot}", 16)
                        stc[jc] = cx.val(f"{fsp}s{slot}")
                cx.fence(ALL_E, ["dve", "act", "htee0", "htee1"]
                         + [f"{fsp}s{i}" for i in range(4)])
                _fe("p%d" % (ip + 1))
            for tt in range(2):
                _fs("p3%s" % "ab"[tt])
                t0 = tt * 512
                hsv = hdram.rearrange("(jc kp) t -> kp jc t", kp=128)
                hsem = ["hha", "hhb", "hhc", "hhd", "hta", "htb", "htc", "htd"]
                hcnt = []
                for ck in range(8):
                    dst = BP.ap()[:, BW + ck * 8 * 512: BW + (ck + 1) * 8 * 512] \
                        .rearrange("p (jc q) -> p jc q", jc=8)
                    srcv = hsv[:, ck * 8:(ck + 1) * 8, t0:t0 + 512]
                    q = ge if ck % 2 == 0 else sy
                    cx.inc(q.dma_start(dst, srcv), hsem[ck], 16)
                    hcnt.append(cx.val(hsem[ck]))
                b0 = cx.base("hpe")
                actc = []
                for c_ in range(64):
                    pb = c_ % 4
                    ck = c_ // 8
                    if c_ % 8 == 0:
                        cx.wait(pe, hsem[ck], hcnt[ck])
                    if c_ >= 4:
                        cx.wait(pe, "act", actc[c_ - 4])
                    mm = pe.matmul(psb(pb), H128b[:], HID(c_),
                                   start=True, stop=False, skip_group_check=True)
                    cx.inc(mm, "hpe", 1)
                    cx.wait(se, "hpe", b0["hpe"] + c_ + 1)
                    i1 = se.copy(HID(c_), psb(pb))
                    cx.inc(i1, "act", 1)
                    actc.append(cx.val("act"))
                cx.fence([pe, ve, se], ["act"])
                wv2 = w2xbf
                xdv = xdram.rearrange("(oc op) t -> oc op t", op=128)
                dstv = dst_dram.rearrange("(oc op) t -> oc op t", op=128)
                b0 = cx.base("sw0", "sw1", "hpe", "hst", "hxi", "hxo")
                dvec = []
                stat_act = []
                for oc in range(16):
                    buf = oc % 2
                    if oc >= 2:
                        cx.wait(sy, "hpe", b0["hpe"] + oc - 1)
                    cx.inc(sy.dma_start(WS[:, buf * 8192: buf * 8192 + 8192],
                                        wv2[oc]), f"sw{buf}", 16)
                    cx.wait(pe, f"sw{buf}", cx.val(f"sw{buf}"))
                    pb = oc % 4
                    if oc >= 4:
                        cx.wait(pe, "dve", dvec[oc - 4])
                    for kc in range(64):
                        lhsT = WS[:, buf * 8192 + kc * 128: buf * 8192 + kc * 128 + 128]
                        mm = pe.matmul(psb(pb), lhsT, HID(kc),
                                       start=(kc == 0), stop=False,
                                       skip_group_check=True)
                    cx.inc(mm, "hpe", 1)
                    cx.wait(sy, "dve")
                    cx.wait(sy, "act")
                    if oc >= 1:
                        cx.wait(sy, "hxo", b0["hxo"] + 16 * oc)
                    cx.inc(sy.dma_start(SC1.ap()[:, 0:512], xdv[oc][:, t0:t0 + 512]),
                           "hxi", 16)
                    cx.wait(ve, "hxi")
                    cx.wait(ve, "hpe", b0["hpe"] + oc + 1)
                    cx.wait(ve, "hxo", b0["hxo"] + 16 * oc)
                    cx.wait(ve, "act")
                    tt_ins = ve.tensor_tensor(SC2.ap()[:, 0:512], psb(pb),
                                              SC1.ap()[:, 0:512], ALU.add)
                    cx.inc(tt_ins, "dve", 1)
                    dvec.append(cx.val("dve"))
                    cx.wait(sy, "dve")
                    cx.inc(sy.dma_start(dstv[oc][:, t0:t0 + 512], SC2.ap()[:, 0:512]),
                           "hxo", 16)
                    if next_ln:
                        cx.wait(se, "dve")
                        cx.inc(se.activation(B[0][:, oc * 512:(oc + 1) * 512],
                                             SC2.ap()[:, 0:512],
                                             AF.Square), "act", 1)
                        cx.inc(se.copy(B[0][:, 8192 + oc * 512: 8192 + (oc + 1) * 512],
                                       SC2.ap()[:, 0:512]), "act", 1)
                        stat_act.append(cx.val("act"))
                if next_ln:
                    for oc in range(16):
                        cx.wait(pe, "act", stat_act[oc])
                        mm = pe.matmul(prow(2048 + tt * 512, 512), LNCb[:, 0:1],
                                       B[0][:, 8192 + oc * 512: 8192 + (oc + 1) * 512],
                                       start=(oc == 0), stop=False,
                                       skip_group_check=True)
                        mm = pe.matmul(prow(3072 + tt * 512, 512), LNCb[:, 1:2],
                                       B[0][:, oc * 512:(oc + 1) * 512],
                                       start=(oc == 0), stop=False,
                                       skip_group_check=True)
                        cx.inc(mm, "hst", 1)
                cx.fence(ALL_E, ["hxo", "hst", "dve", "act"])
                _fe("p3%s" % "ab"[tt])
            _lns(tag)

        ln_apply(B[0])
        w2x_prep()
        ffn_eval("f1", B[0], first_eval=True, dst_dram=xdram, next_ln=True)

        if PHASE_LIMIT == 2:
            finalize_trunk()
            return nc

        ln_apply(B[0])
        XN3 = lambda kc: B[0][:, kc * T:(kc + 1) * T]
        Qh = lambda h: B[1][:, h * T:(h + 1) * T]
        qkvv = E["wqkv_qk"]
        kinv = ag_kin[0:2048, :].rearrange("(oc op) t -> oc op t", op=128)
        pfence()
        _ens("qk")
        b0 = cx.base("hw0", "hw1", "hpe", "hst")
        actc = []
        QS = lambda s: B[2][:, s * 1024:(s + 1) * 1024]
        for j in range(32):
            buf = j % 2
            if j >= 2:
                cx.wait(ge, "hpe", b0["hpe"] + j - 1)
            cx.inc(ge.dma_start(WS[:, buf * 8192: buf * 8192 + 2048],
                                qkvv[j]), f"hw{buf}", 16)
            cx.wait(pe, f"hw{buf}", cx.val(f"hw{buf}"))
            if j >= 2:
                cx.wait(pe, "act", actc[j - 2])
            pb = (j % 2) * 2
            for kc in range(KC):
                lhsT = WS[:, buf * 8192 + kc * 128: buf * 8192 + kc * 128 + 128]
                for i in range(2):
                    mm = pe.matmul(psb(pb + i), lhsT,
                                   XN3(kc)[:, i * 512: i * 512 + 512],
                                   start=(kc == 0), stop=False,
                                   skip_group_check=True)
            cx.inc(mm, "hpe", 1)
            cx.wait(se, "hpe", b0["hpe"] + j + 1)
            if j >= 2:
                cx.wait(se, "hst", b0["hst"] + j - 1)
            if j < 16:
                i1 = se.activation(Qh(j), psb(pb, 1024), AF.Identity,
                                   bias=CW["bqkv_c"][:, j:j + 1])
                cx.inc(i1, "act", 1)
            else:
                sl = buf * 1032
                cx.wait(se, f"kx{buf}", cx.val(f"kx{buf}"))
                i1 = se.activation(SBF.ap()[:, sl:sl + 1024], psb(pb, 1024),
                                   AF.Identity, bias=CW["bqkv_c"][:, j:j + 1])
                cx.inc(i1, "act", 1)
                cx.wait(sy, "act")
                cx.inc(sy.dma_start(kinv[j - 16], SBF.ap()[:, sl:sl + 1024]),
                       f"kx{buf}", 16)
            i2 = se.activation(QS(buf), psb(pb, 1024), AF.Square)
            cx.inc(i2, "act", 1)
            actc.append(cx.val("act"))
            cx.wait(pe, "act", actc[j])
            for i in range(2):
                mm = pe.matmul(pcmp((4 if j < 16 else 6) * 512 + i * 512, 512),
                               csm(j % 16), QS(buf)[:, i * 512: i * 512 + 512],
                               start=(j % 16 == 0), stop=False,
                               skip_group_check=True)
            cx.inc(mm, "hst", 1)
        cx.wait(se, "hst", b0["hst"] + 32)
        cx.inc(se.copy(CPa[:], pcmp(2048, 1024)), "act", 1)
        cx.inc(se.copy(CPb[:], pcmp(3072, 1024)), "act", 1)
        cx.wait(ve, "act")
        cx.wait(ve, "kx0")
        cx.wait(ve, "kx1")
        cx.inc(ve.tensor_copy(SBF.ap()[0:16, 0:1024], CPb[:]), "dve", 1)
        cx.wait(sy, "dve")
        cx.inc(sy.dma_start(ag_kin[4096:4112, :], SBF.ap()[0:16, 0:1024]),
               "kx0", 16)
        cx.fence(ALL_E, ["kx0", "kx1", "act", "dve"])
        _lns("qk")

        BQKVRb = RBF.ap()[0:1, 0:2048]
        cx.inc(ge.dma_start(BQKVRb, E["bqkv_r"][:]), "inig", 16)
        cx.fence([pe], ["inig"])
        _ens("vt")
        vTv = ag_kin[2048:4096, :].rearrange("(t two) q -> t (two q)", two=2)
        b0 = cx.base("hw0", "hw1", "hpe", "hxo")
        actc = []
        idx = 0
        for vc in range(8):
            buf = vc % 2
            if vc >= 2:
                cx.wait(ge, "hpe", b0["hpe"] + (vc - 1) * 8)
            cx.inc(ge.dma_start(WS[:, buf * 8192: buf * 8192 + 4096],
                                E["wqkv_v"][vc]), f"hw{buf}", 16)
            cx.wait(pe, f"hw{buf}", cx.val(f"hw{buf}"))
            for tb in range(8):
                pb = idx % 4
                if idx >= 4:
                    cx.wait(pe, "act", actc[idx - 4])
                for kc in range(KC):
                    lhsT = XN3(kc)[:, tb * 128:(tb + 1) * 128]
                    mm = pe.matmul(psb(pb, 256), lhsT,
                                   WS[:, buf * 8192 + kc * 256: buf * 8192 + kc * 256 + 256],
                                   start=(kc == 0), stop=False,
                                   skip_group_check=True)
                mm = pe.matmul(psb(pb, 256), ONESb[0:1, 0:128],
                               BQKVRb[0:1, vc * 256: vc * 256 + 256],
                               start=False, stop=False, skip_group_check=True)
                cx.inc(mm, "hpe", 1)
                cx.wait(se, "hpe", b0["hpe"] + idx + 1)
                sl = (idx % 2) * 1032
                cx.wait(se, f"kx{idx % 2}", cx.val(f"kx{idx % 2}"))
                i1 = se.activation(SBF.ap()[:, sl:sl + 256], psb(pb, 256), AF.Identity)
                cx.inc(i1, "act", 1)
                actc.append(cx.val("act"))
                cx.wait(sy, "act")
                cx.inc(sy.dma_start(vTv[tb * 128:(tb + 1) * 128,
                                        vc * 256: vc * 256 + 256],
                                    SBF.ap()[:, sl:sl + 256]), f"kx{idx % 2}", 16)
                idx += 1
        cx.fence(ALL_E, ["kx0", "kx1", "act"])

        _lns("vt")
        _ens("exch")
        cx.inc(ge.collective_compute(
            "AllGather", ALU.bypass, ins=[ag_kin.opt()], outs=[ag_kout.opt()],
            replica_groups=[[0, 1], [2, 3], [4, 5], [6, 7]]), "cc", 1)
        cx.fence(ALL_E, ["cc"])
        cx.inc(sy.dma_start(SBF.ap()[0:16, 0:1024], ag_kout[4096:4112, :]), "ag2", 16)
        cx.inc(sy.dma_start(SBF.ap()[0:16, 1032:1032 + 1024],
                            ag_kout[8208:8224, :]), "ag2", 16)
        cx.wait(ve, "ag2")
        cx.inc(ve.tensor_tensor(SBF.ap()[0:16, 0:1024], SBF.ap()[0:16, 0:1024],
                                SBF.ap()[0:16, 1032:1032 + 1024], ALU.max), "dve", 1)
        cx.wait(ve, "dve")
        cx.inc(ve.reduce_max(CPb[:, 0:1], SBF.ap()[0:16, 0:1024], axis=AX.X),
               "dve", 1)
        cx.inc(ve.reduce_max(CPa[:, 0:1], CPa[:], axis=AX.X), "dve", 1)
        cx.wait(se, "dve")
        cx.inc(se.activation(CPa[:, 0:1], CPa[:, 0:1], AF.Ln, bias=np.float32(1e-30)),
               "act", 1)
        cx.inc(se.activation(CPb[:, 0:1], CPb[:, 0:1], AF.Ln, bias=np.float32(1e-30)),
               "act", 1)
        cx.wait(ve, "act")
        cx.inc(ve.tensor_add(CPa[:, 0:1], CPa[:, 0:1], CPb[:, 0:1]), "dve", 1)
        cx.wait(se, "dve")
        cx.inc(se.activation(CPa[:, 0:1], CPa[:, 0:1], AF.Exp, scale=np.float32(0.5)),
               "act", 1)
        cx.wait(ve, "act")
        cx.inc(ve.tensor_scalar(CPa[:, 0:1], CPa[:, 0:1],
                                float(-np.sqrt(128.0)), None, ALU.mult), "dve", 1)
        mrow = nc.dram_tensor("mrow", [16, 1], F32, kind="Internal").ap()
        cx.wait(sy, "dve")
        cx.inc(sy.dma_start(mrow, CPa[:, 0:1]), "mx", 16)
        cx.wait(sy, "mx")
        cx.inc(sy.dma_start(R1[0:1, 0:16], mrow.rearrange("a b -> b a")), "mx", 16)
        cx.wait(ve, "mx")
        cx.inc(ve.tensor_copy(RBF.ap()[0:1, 2048:2064], R1[0:1, 0:16]), "dve", 1)
        cx.wait(pe, "dve")
        mm = pe.matmul(pcol(2048, 16), ONESb[0:1, 0:128], RBF.ap()[0:1, 2048:2064],
                       start=True, stop=False, skip_group_check=True)
        cx.inc(mm, "pe", 1)
        cx.wait(se, "pe")
        cx.inc(se.copy(MHall.ap(), pcol(2048, 16)), "act", 1)
        cx.wait(se, "act")
        _lns("exch")

        _ens("heads")
        AOUT = lambda h: B[3][:, h * T:(h + 1) * T]
        EST = lambda s: B[2][:, s * T:(s + 1) * T]
        vT0 = ag_kout[2048:4096, :].rearrange("(t two) q -> t (two q)", two=2)
        vT1 = ag_kout[6160:8208, :].rearrange("(t two) q -> t (two q)", two=2)
        bhh = cx.base("hev")
        pecnt = [bhh["hev"]]

        def head_loads(h):
            hb = (h % 2) * 8192
            if h >= 2:
                cx.wait(sy, "hev", pecnt[h - 1])
            cx.inc(sy.dma_start(WS[:, hb:hb + 1024],
                                ag_kout[h * 128:(h + 1) * 128, :]), f"hk{h % 2}", 16)
            cx.inc(sy.dma_start(WS[:, hb + 1024:hb + 2048],
                                ag_kout[4112 + h * 128: 4112 + h * 128 + 128, :]),
                   f"hk{h % 2}", 16)
            cx.inc(sy.dma_start(
                WS.ap()[:, hb + 2048:hb + 3072].rearrange("p (tb q) -> p tb q", tb=8),
                vT0.rearrange("(tb p) q -> tb p q", p=128)[:, :, h * 128:(h + 1) * 128]),
                f"hk{h % 2}", 16)
            cx.inc(sy.dma_start(
                WS.ap()[:, hb + 3072:hb + 4096].rearrange("p (tb q) -> p tb q", tb=8),
                vT1.rearrange("(tb p) q -> tb p q", p=128)[:, :, h * 128:(h + 1) * 128]),
                f"hk{h % 2}", 16)

        head_loads(0)
        prev_act = None
        rbf_dve = None
        evac_act = [None]
        tt_dve = [None]
        last_bcast_pe = cx.val("pe")

        def tail_b(hprev):
            nonlocal last_bcast_pe
            cx.wait(pe, "dve", rbf_dve)
            for i in range(2):
                mm = pe.matmul(pcol(2048 + i * 512, 512), ONESb[0:1, 0:128],
                               RBF.ap()[0:1, i * 512: i * 512 + 512],
                               start=True, stop=False, skip_group_check=True)
            cx.inc(mm, "pe", 1)
            last_bcast_pe = cx.val("pe")
            cx.wait(se, "pe")
            if hprev >= 1 and tt_dve[hprev - 1] is not None:
                cx.wait(se, "dve", tt_dve[hprev - 1])
            cx.inc(se.copy(SBF.ap()[:, 0:1024], pcol(2048, 1024)), "act", 1)
            evac_act.append(cx.val("act"))
            cx.wait(ve, "act", evac_act[-1])
            cx.wait(ve, "hev", pecnt[hprev + 1])
            cx.inc(ve.tensor_tensor(AOUT(hprev), pcol(3072, 1024),
                                    SBF.ap()[:, 0:1024], ALU.mult), "dve", 1)
            tt_dve.append(cx.val("dve"))

        for h in range(16):
            hb = (h % 2) * 8192
            bh = cx.base("hpe", "hev")
            actc2 = []
            cx.wait(pe, f"hk{h % 2}", cx.val(f"hk{h % 2}"))

            def ST(tb):
                pb = (tb % 2) * 2
                if tb >= 2:
                    cx.wait(pe, "act", actc2[tb - 2])
                elif prev_act is not None:
                    cx.wait(pe, "act", prev_act[14 + tb])
                kt = WS[:, hb + tb * 128: hb + (tb + 1) * 128] if tb < 8 else \
                    WS[:, hb + 1024 + (tb - 8) * 128: hb + 1024 + (tb - 7) * 128]
                for i in range(2):
                    mm = pe.matmul(psb(pb + i), kt,
                                   Qh(h)[:, i * 512: i * 512 + 512],
                                   start=True, stop=False, skip_group_check=True)
                cx.inc(mm, "hpe", 1)
                cx.wait(se, "hpe", bh["hpe"] + tb + 1)
                if tb >= 2:
                    cx.wait(se, "hev", bh["hev"] + tb - 1)
                i1 = se.activation(EST(tb % 2), psb(pb, 1024), AF.Exp,
                                   bias=MHall.ap()[:, h:h + 1],
                                   scale=np.float32(RSQ))
                cx.inc(i1, "act", 1)
                actc2.append(cx.val("act"))

            def SPV(tb):
                cx.wait(pe, "act", actc2[tb])
                if tb == 0 and evac_act[h] is not None:
                    cx.wait(pe, "act", evac_act[h])
                if tb == 0 and tt_dve[h] is not None:
                    cx.wait(pe, "dve", tt_dve[h])
                for i in range(2):
                    mm = pe.matmul(prow(2048 + i * 512, 512), LNCb[:, 2:3],
                                   EST(tb % 2)[:, i * 512: i * 512 + 512],
                                   start=(tb == 0), stop=False,
                                   skip_group_check=True)
                vt = WS[:, hb + 2048 + tb * 128: hb + 2048 + (tb + 1) * 128] \
                    if tb < 8 else \
                    WS[:, hb + 3072 + (tb - 8) * 128: hb + 3072 + (tb - 7) * 128]
                for i in range(2):
                    mm = pe.matmul(pcol(3072 + i * 512, 512), vt,
                                   EST(tb % 2)[:, i * 512: i * 512 + 512],
                                   start=(tb == 0), stop=False,
                                   skip_group_check=True)
                cx.inc(mm, "hev", 1)

            ST(0)
            if h + 1 < 16:
                head_loads(h + 1)
            ST(1)
            if h >= 1:
                tail_b(h - 1)
            ST(2)
            SPV(0)
            for tb in range(3, 16):
                ST(tb)
                SPV(tb - 2)
            SPV(14)
            SPV(15)
            pecnt.append(cx.val("hev"))
            prev_act = actc2
            cx.wait(se, "hev", bh["hev"] + 16)
            if rbf_dve is not None:
                cx.wait(se, "dve", rbf_dve)
            cx.inc(se.activation(R1[0:1, 0:1024], prow(2048, 1024), AF.Ln,
                                 bias=np.float32(1e-38)), "act", 1)
            cx.wait(se, "act")
            cx.inc(se.activation(R1[0:1, 0:1024], R1[0:1, 0:1024], AF.Exp,
                                 scale=np.float32(-1.0)), "act", 1)
            cx.wait(ve, "act")
            cx.wait(ve, "pe", last_bcast_pe)
            cx.inc(ve.tensor_copy(RBF.ap()[0:1, 0:1024], R1[0:1, 0:1024]), "dve", 1)
            rbf_dve = cx.val("dve")
        tail_b(15)
        cx.fence(ALL_E, ["dve", "act"])
        _lns("heads")

        with nc.named_scope("proj"):
            residual_proj("pj", E["wproj"], AOUT,
                          lambda oc: xdram.rearrange("(oc op) t -> oc op t", op=128)[oc],
                          xdram, bias_cols=CW["bproj"], next_ln=True)

        if PHASE_LIMIT == 3:
            finalize_trunk()
            return nc

        ln_apply(B[0])
        ffn_eval("f2", B[0], first_eval=False, dst_dram=y_out, next_ln=False)

    return nc
